# version 18
# baseline (speedup 1.0000x reference)
import bisect
import contextlib
import numpy as np
import ml_dtypes

import concourse.bass as bass
import concourse.mybir as mybir
from concourse.bass_utils import run_bass_kernel_spmd

F32 = mybir.dt.float32
BF16 = mybir.dt.bfloat16
AF = mybir.ActivationFunctionType
OP = mybir.AluOpType
BFNP = ml_dtypes.bfloat16

RC = float(np.float32(1.5 * 2 ** 23))
CONSTS = (1e-5, 1.0 / 64.0, 16.0, 1.0 / 1024.0, -1.0, 1.0)

P = 128
D = 1024
DI = 2048
NST = 16
L = 2048
TB = 1024
NT = TB // P
DCH = DI // P
KCH = D // P
FC = 32
HC = 32

STREAMS = ("PE", "ACT", "DVE", "POOL", "SP")
DEBUG_TAPS = False


class Buf:
    __slots__ = ("name", "writer", "readers")

    def __init__(self, name):
        self.name = name
        self.writer = None
        self.readers = {}


class Op:
    __slots__ = ("fn", "waits", "inc_val", "tag", "dma", "dma_sem")

    def __init__(self, fn, tag, dma=False):
        self.fn = fn
        self.waits = []
        self.inc_val = None
        self.tag = tag
        self.dma = dma
        self.dma_sem = None


NDMASEM = 16


class Sched:
    def __init__(self, nc, sems):
        self.nc = nc
        self.sems = sems
        self.ops = {s: [] for s in STREAMS}
        self.incs = {s: [] for s in STREAMS}
        self.counts = {s: 0 for s in STREAMS}
        self.waited = {s: {} for s in STREAMS}
        self.dma_count = 0
        self.dma_uses = [0] * NDMASEM
        self.barrier_levels = {}

    def _resolve(self, src, op_idx):
        lst = self.incs[src]
        keys = [i for i, _ in lst]
        j = bisect.bisect_left(keys, op_idx)
        if j < len(lst):
            return lst[j][1]
        ops = self.ops[src]
        last = len(ops) - 1
        while last >= op_idx and ops[last].dma:
            last -= 1
        if last < op_idx:
            ops.append(Op(lambda eng: eng.nop(), "nop-inc"))
            last = len(ops) - 1
        self.counts[src] += 1
        v = self.counts[src]
        ops[last].inc_val = v
        lst.append((last, v))
        return v

    def _dep_val(self, dep):
        src, x = dep
        return (src, x if src == "DMA" else self._resolve(src, x))

    def op(self, stream, fn, reads=(), writes=(), is_dma=False, tag=""):
        deps = {}

        def add(dep, raw=False):
            if dep is None:
                return
            src, x = dep
            if src == stream and not raw:
                return
            if src not in deps or deps[src] < x:
                deps[src] = x

        for b in reads:
            add(b.writer, raw=True)
        for b in writes:
            add(b.writer)
            for src, x in b.readers.items():
                add((src, x))
        o = Op(fn, tag, dma=is_dma)
        idx = len(self.ops[stream])
        wl = self.waited[stream]

        def want(src, val):
            if wl.get(src, 0) >= val:
                return
            wl[src] = val
            o.waits.append((src, val))

        if is_dma:
            k = self.dma_count % NDMASEM
            self.dma_count += 1
            if self.dma_uses[k] > 0:
                add(("DMA%d" % k, 16 * self.dma_uses[k]))
            self.dma_uses[k] += 1
            o.dma_sem = "DMA%d" % k
        for src, x in deps.items():
            want(src, x if src.startswith("DMA") else self._resolve(src, x))
        for src, val in self.barrier_levels.items():
            if src != stream:
                want(src, val)
        self.ops[stream].append(o)
        if is_dma:
            dep = (o.dma_sem, 16 * self.dma_uses[(self.dma_count - 1) % NDMASEM])
        else:
            dep = (stream, idx)
        for b in writes:
            b.writer = dep
            b.readers = {}
        key, rv = dep if is_dma else (stream, idx)
        for b in reads:
            if b.readers.get(key, -1) < rv:
                b.readers[key] = rv
        return dep

    def dma(self, stream, fn, reads=(), writes=(), tag=""):
        return self.op(stream, fn, reads=reads, writes=writes, is_dma=True,
                       tag=tag)

    def alias(self, dst_bufs, src_bufs):
        for d in dst_bufs:
            for s in src_bufs:
                if s is d:
                    continue
                deps = list(s.readers.items())
                if s.writer is not None:
                    deps.append(s.writer)
                for src, x in deps:
                    if d.readers.get(src, -1) < x:
                        d.readers[src] = x

    def barrier(self):
        lv = {}
        for s in STREAMS:
            if self.ops[s]:
                lv[s] = self._resolve(s, len(self.ops[s]) - 1)
        for k in range(NDMASEM):
            if self.dma_uses[k]:
                lv["DMA%d" % k] = 16 * self.dma_uses[k]
        self.barrier_levels = lv

    def emit(self, block):
        engines = {
            "PE": block.tensor, "ACT": block.scalar, "DVE": block.vector,
            "POOL": block.gpsimd, "SP": block.sync,
        }
        nc = self.nc
        engmap = {
            "PE": nc.tensor, "ACT": nc.scalar, "DVE": nc.vector,
            "POOL": nc.gpsimd, "SP": nc.sync,
        }

        def make(stream):
            def body(_eng):
                eng = engmap[stream]
                sem_self = self.sems[stream]
                for o in self.ops[stream]:
                    for src, val in o.waits:
                        eng.wait_ge(self.sems[src], val)
                    inst = o.fn(eng)
                    if o.dma:
                        inst.then_inc(self.sems[o.dma_sem], 16)
                    if o.inc_val is not None:
                        assert inst is not None, (stream, o.tag)
                        inst.then_inc(sem_self, 1)
            return body

        for s in STREAMS:
            engines[s](make(s))



def _stage(inputs):
    f32 = np.float32
    x = np.ascontiguousarray(inputs["L0_attn_x"], dtype=f32)
    W_in = np.asarray(inputs["L0_attn_W_in"], f32)
    conv_w = np.asarray(inputs["L0_attn_conv_w"], f32)
    conv_b = np.asarray(inputs["L0_attn_conv_b"], f32)
    W_x = np.asarray(inputs["L0_attn_W_x"], f32)
    W_dt = np.asarray(inputs["L0_attn_W_dt"], f32)
    b_dt = np.asarray(inputs["L0_attn_b_dt"], f32)
    W_out = np.asarray(inputs["L0_attn_W_out"], f32)
    log_A = np.asarray(inputs["L0_attn_log_A"], f32)
    ln_g = np.asarray(inputs["L0_attn_ln_g"], f32)
    ln_b = np.asarray(inputs["L0_attn_ln_b"], f32)
    w1 = np.asarray(inputs["L0_ffn_w1"], f32)
    b1 = np.asarray(inputs["L0_ffn_b1"], f32)
    w2 = np.asarray(inputs["L0_ffn_w2"], f32)
    b2 = np.asarray(inputs["L0_ffn_b2"], f32)

    win_s = np.ascontiguousarray(
        W_in.reshape(D, FC, P).transpose(1, 0, 2))
    w1t_s = np.ascontiguousarray(
        w1.reshape(HC, P, D).transpose(0, 2, 1))
    w2t = np.ascontiguousarray(w2.T)

    convdiag = np.zeros((P, 64 * 32), f32)
    j = np.arange(P)
    for dc in range(DCH):
        for k in range(4):
            convdiag[j, (dc * 4 + k) * 32 + (j % 32)] = conv_w[dc * P + j, 0, k]

    def tcol(v, n):
        return np.ascontiguousarray(v.reshape(n, P).T)

    loga_t = np.ascontiguousarray(
        log_A.reshape(DCH, P, NST).transpose(1, 0, 2).reshape(P, DCH * NST))

    common = dict(
        win=win_s, wx=np.ascontiguousarray(W_x), wdt=np.ascontiguousarray(W_dt),
        wout=np.ascontiguousarray(W_out), w1t=w1t_s, w2t=w2t,
        convdiag=convdiag, cvb=tcol(conv_b, DCH), bdt=tcol(b_dt, DCH),
        loga=loga_t, lng=tcol(ln_g, KCH), lnb=tcol(ln_b, KCH),
        b1=tcol(b1, HC), b2=np.ascontiguousarray(b2[None, :]),
        eye=np.eye(P, dtype=f32),
    )
    shards = []
    for c in range(8):
        m = dict(common)
        m["x"] = np.ascontiguousarray(x[2 * c:2 * c + 2])
        shards.append(m)
    return shards



def build_program():
    nc = bass.Bass("TRN2", target_bir_lowering=False)
    ex = contextlib.ExitStack()

    def din(name, shape, dt=F32, kind="ExternalInput"):
        return nc.dram_tensor(name, shape, dt, kind=kind).ap()

    t_x = din("x", [2, L, D])
    t_win = din("win", [FC, D, P])
    t_wx = din("wx", [DI, 33])
    t_wdt = din("wdt", [1, DI])
    t_wout = din("wout", [DI, D])
    t_w1t = din("w1t", [HC, D, P])
    t_w2t = din("w2t", [4 * D, D])
    t_cdiag = din("convdiag", [P, 64 * 32])
    t_cvb = din("cvb", [P, DCH])
    t_bdt = din("bdt", [P, DCH])
    t_loga = din("loga", [P, DCH * NST])
    t_lng = din("lng", [P, KCH])
    t_lnb = din("lnb", [P, KCH])
    t_b1 = din("b1", [P, HC])
    t_b2 = din("b2", [1, D])
    t_eye = din("eye", [P, P])
    t_out = din("out", [2, L, D], kind="ExternalOutput")
    t_dbg = {}
    if DEBUG_TAPS:
        for nm, shp, dt in [
            ("dbg_xnfm", [P, KCH * 1024], BF16),
            ("dbg_xi", [P, DCH * 1028], BF16),
            ("dbg_sz", [P, DCH * 1024], BF16),
            ("dbg_dbc", [P, 1024], BF16),
            ("dbg_delta0", [P, 8 * 1024], BF16),
            ("dbg_u", [P, DCH * 1028], BF16),
            ("dbg_y", [P, DCH * 1024], BF16),
            ("dbg_x1", [P, 8 * 1024], BF16),
        ]:
            t_dbg[nm] = din(nm, shp, dt, kind="ExternalOutput")
    t_winbf = din("winbf", [FC, D, P], BF16, kind="Internal")
    t_w1q = din("w1q", [HC, D, P], BF16, kind="Internal")
    t_w2q = din("w2q", [4 * D, D], BF16, kind="Internal")
    t_woutbf = din("woutbf", [DI, D], BF16, kind="Internal")

    def sb(name, shape, dt):
        return ex.enter_context(nc.sbuf_tensor(name, shape, dt))

    r_wx = sb("r_wx", [P, DCH * 33], BF16)
    r_cdiag = sb("r_cdiag", [P, 64 * 32], BF16)
    r_A = sb("r_A", [P, DCH * NST], F32)
    r_wdt = sb("r_wdt", [1, DI], BF16)
    r_cvb = sb("r_cvb", [P, DCH], F32)
    r_bdt = sb("r_bdt", [P, DCH], F32)
    r_lng = sb("r_lng", [P, KCH], F32)
    r_lnb = sb("r_lnb", [P, KCH], F32)
    r_b1q4 = sb("r_b1q4", [P, HC], F32)
    r_b2qrep = sb("r_b2qrep", [P, D], BF16)
    r_eye = sb("r_eye", [P, P], F32)
    r_consts = sb("r_consts", [P, 8], F32)
    r_ones = sb("r_ones", [1, P], BF16)
    r_carry = sb("r_carry", [P, DCH * NST], F32)
    r_halo = sb("r_halo", [P, DCH * 3], BF16)
    w_win = sb("w_win", [P, 2 * KCH * P], BF16)
    w_w1 = sb("w_w1", [P, 4 * KCH * P], BF16)
    w_w2 = sb("w_w2", [P, 4 * D], BF16)
    w_wout = sb("w_wout", [P, 4 * D], BF16)
    xi_sb = sb("xi_sb", [P, DCH * 1028], BF16)
    sz_sb = sb("sz_sb", [P, DCH * 1024], BF16)
    xnfm = sb("xnfm", [P, KCH * 1024], BF16)
    dbc = sb("dbc", [P, 1024], BF16)
    bcB0 = sb("bcB0", [1, 1024], BF16)
    bcC0 = sb("bcC0", [1, 1024], BF16)
    slotA = sb("slotA", [P, 16 * 1024], BF16)
    slotB = sb("slotB", [P, 8 * 1024], BF16)
    slotC = sb("slotC", [P, 8 * 1024], BF16)
    slotD = sb("slotD", [P, 3072], F32)
    stt = sb("stt", [P, 8 * 16], F32)
    rrs = sb("rrs", [P, 8], F32)
    mu1a = sb("mu1a", [P, NT], F32)
    r1a = sb("r1a", [P, NT], F32)
    psum = [ex.enter_context(nc.psum_tensor(f"ps{i}", [P, 512], F32))
            for i in range(8)]

    sems = {}
    for s in list(STREAMS) + ["DMA%d" % i for i in range(NDMASEM)]:
        sems[s] = ex.enter_context(nc.semaphore("sem_" + s))
    block = ex.enter_context(nc.Block())
    S = Sched(nc, sems)

    B = lambda n: Buf(n)
    b_wx, b_cdiag, b_A, b_wdt = B("wx"), B("cdiag"), B("A"), B("wdt")
    b_cvb, b_bdt, b_lng, b_lnb = B("cvb"), B("bdt"), B("lng"), B("lnb")
    b_b1q4, b_b2qrep, b_eye, b_ones = B("b1q4"), B("b2qrep"), B("eye"), B("ones")
    b_carry, b_halo = B("carry"), B("halo")
    b_wwin = [B("wwin0"), B("wwin1")]
    b_ww1 = [B(f"ww1{i}") for i in range(4)]
    b_ww2 = [B(f"ww2{i}") for i in range(4)]
    b_wwout = [B(f"wwout{i}") for i in range(4)]
    b_xic = [B(f"xi{i}") for i in range(DCH)]
    b_szc = [B(f"sz{i}") for i in range(DCH)]
    b_xnfm, b_dbc = B("xnfm"), B("dbc")
    b_bcB = [B("bcB0")]
    b_bcC = [B("bcC0")]
    b_slotA, b_slotB = B("slotA"), B("slotB")
    b_dlt = [B(f"dlt{i}") for i in range(8)]
    b_xn2f = B("xn2f")
    b_dA2 = [B("dA0"), B("dA1")]
    b_dB2 = [B("dB0"), B("dB1")]
    b_h2 = [B("h0"), B("h1")]
    b_tmp2 = [B("tmp0"), B("tmp1")]
    b_Brep = [B("Brep0"), B("Brep1")]
    b_Crep = [B("Crep0"), B("Crep1")]
    b_x1 = B("x1")
    b_xt = [B("xt0"), B("xt1")]
    b_xn1 = B("xn1")
    b_faH = [B("fa0"), B("fa1")]
    b_ob = B("ob")
    b_spA = [B("spA0"), B("spA1")]
    b_spB = [B("spB0"), B("spB1")]
    b_stat = [B(f"stat{i}") for i in range(8)]
    b_rrs = [B(f"rrs{i}") for i in range(8)]
    b_mu1a, b_r1a = B("mu1a"), B("r1a")
    b_ps = [B(f"ps{i}") for i in range(8)]
    b_winbf, b_w1q, b_w2q, b_woutbf = B("winbf"), B("w1q"), B("w2q"), B("woutbf")
    b_stD = [B(f"stD{i}") for i in range(3)]
    b_stA = [B(f"stA{i}") for i in range(4)]

    cA = lambda i: slotC[:, i * 1024:(i + 1) * 1024]
    dA_ = lambda i: slotD[:, i * 1024:(i + 1) * 1024]

    def P_dma(dst, src, writes, reads=(), tag=""):
        S.dma("SP", lambda eng, d=dst, s_=src: eng.dma_start(out=d, in_=s_),
              reads=reads, writes=writes, tag=tag)

    def P_act(out, in_, func, bias=0.0, scale=1.0, reads=(), writes=(), tag=""):
        S.op("ACT", lambda eng, o=out, i=in_, f=func, b=bias, sc=scale:
             eng.activation(out=o, in_=i, func=f, bias=b, scale=sc),
             reads=reads, writes=writes, tag=tag)

    def P_ts(out, in0, s1, s2, op0, op1=None, stream="DVE", reads=(), writes=(),
             tag=""):
        if s2 is None:
            S.op(stream, lambda eng, o=out, i=in0, a=s1, x=op0:
                 eng.tensor_single_scalar(out=o, in_=i, scalar=a, op=x),
                 reads=reads, writes=writes, tag=tag)
        else:
            S.op(stream, lambda eng, o=out, i=in0, a=s1, b=s2, x=op0, y=op1:
                 eng.tensor_scalar(out=o, in0=i, scalar1=a, scalar2=b, op0=x,
                                   op1=y),
                 reads=reads, writes=writes, tag=tag)

    def P_tt(out, in0, in1, op, reads=(), writes=(), tag=""):
        S.op("DVE", lambda eng, o=out, a=in0, b=in1, x=op:
             eng.tensor_tensor(o, a, b, x), reads=reads, writes=writes, tag=tag)

    def P_copy(out, in_, stream="DVE", reads=(), writes=(), tag=""):
        S.op(stream, lambda eng, o=out, i=in_: eng.tensor_copy(o, i),
             reads=reads, writes=writes, tag=tag)

    def P_mm(out, lhsT, rhs, start, stop, reads=(), writes=(), tag=""):
        S.op("PE", lambda eng, o=out, l=lhsT, r=rhs, st=start, sp=stop:
             eng.matmul(o, l, r, start=st, stop=sp),
             reads=reads, writes=writes, tag=tag)

    def P_tp(out, in_, reads=(), writes=(), tag=""):
        S.op("PE", lambda eng, o=out, i=in_: eng.transpose(o, i, r_eye[:, :]),
             reads=list(reads) + [b_eye], writes=writes, tag=tag)

    b_consts = B("consts")
    for ci, cv in enumerate(CONSTS):
        S.op("DVE", lambda eng, i=ci, v=cv: eng.memset(r_consts[:, i:i + 1], v),
             writes=[b_consts])
        nc.const_aps.register(F32, cv, r_consts[:, ci:ci + 1])
    P_dma(r_eye[:, :], t_eye, [b_eye])
    S.op("DVE", lambda eng: eng.memset(r_ones[:, :], 1.0), writes=[b_ones])
    P_dma(dA_(0)[:, :DCH * NST], t_loga, [b_stD[0]], reads=[b_stD[0]])
    P_act(dA_(1)[:, :DCH * NST], dA_(0)[:, :DCH * NST], AF.Exp,
          reads=[b_stD[0]], writes=[b_stD[0]])
    P_ts(r_A[:, :], dA_(1)[:, :DCH * NST], -1.0, None, OP.mult,
         reads=[b_stD[0]], writes=[b_A])
    P_dma(r_cvb[:, :], t_cvb, [b_cvb])
    P_dma(r_bdt[:, :], t_bdt, [b_bdt])
    P_dma(r_lng[:, :], t_lng, [b_lng])
    P_dma(r_lnb[:, :], t_lnb, [b_lnb])
    P_dma(dA_(0)[:, :HC], t_b1, [b_stD[0]], reads=[b_stD[0]])
    P_ts(dA_(1)[:, :HC], dA_(0)[:, :HC], 64.0, RC, OP.mult, OP.add,
         reads=[b_stD[0]], writes=[b_stD[0]])
    P_ts(dA_(1)[:, :HC], dA_(1)[:, :HC], RC, 0.25, OP.subtract, OP.mult,
         reads=[b_stD[0]], writes=[b_stD[0]])
    P_ts(r_b1q4[:, :], dA_(1)[:, :HC], -32.0, 31.75, OP.max, OP.min,
         reads=[b_stD[0]], writes=[b_b1q4])
    P_dma(dA_(0)[0:1, :D], t_b2, [b_stD[0]], reads=[b_stD[0]])
    P_ts(dA_(1)[0:1, :D], dA_(0)[0:1, :D], 64.0, RC, OP.mult, OP.add,
         reads=[b_stD[0]], writes=[b_stD[0]])
    P_ts(dA_(1)[0:1, :D], dA_(1)[0:1, :D], RC, 1.0 / 64.0, OP.subtract, OP.mult,
         reads=[b_stD[0]], writes=[b_stD[0]])
    P_ts(r_b2qrep[0:1, :], dA_(1)[0:1, :D], -2.0, 127.0 / 64.0, OP.max, OP.min,
         reads=[b_stD[0]], writes=[b_b2qrep])
    pp = 1
    while pp < P:
        P_dma(r_b2qrep[pp:min(2 * pp, P), :], r_b2qrep[0:min(pp, P - pp), :],
              [b_b2qrep], reads=[b_b2qrep])
        pp *= 2
    P_dma(dA_(0)[:, :DCH * 33].rearrange("p (kc j) -> p kc j", j=33),
          t_wx.rearrange("(kc p) j -> p kc j", p=P), [b_stD[0]], reads=[b_stD[0]])
    P_act(r_wx[:, :], dA_(0)[:, :DCH * 33], AF.Identity,
          reads=[b_stD[0]], writes=[b_wx])
    P_dma(slotD[0:1, :DI], t_wdt, [b_stD[0]], reads=[b_stD[0]])
    P_act(r_wdt[:, :], slotD[0:1, :DI], AF.Identity,
          reads=[b_stD[0]], writes=[b_wdt])
    P_dma(slotD[:, :2048], t_cdiag[:, :], [b_stD[0]], reads=[b_stD[0]])
    P_act(r_cdiag[:, :], slotD[:, :2048], AF.Identity,
          reads=[b_stD[0]], writes=[b_cdiag])
    def stD_ap(i, n):
        r = i % 3
        return slotD[:, r * 1024: r * 1024 + n], b_stD[r]

    def stA_ap(i, n):
        r = i % 4
        return slotA[:, r * 2048: r * 2048 + n], b_stA[r]

    for g in range(16):
        src_ = t_win[2 * g:2 * g + 2].rearrange("fc (kc p) f -> p fc kc f", p=P)
        dst = t_winbf[2 * g:2 * g + 2].rearrange("fc (kc p) f -> p fc kc f", p=P)
        fap, fb = stD_ap(g, 1024)
        aap_, ab = stA_ap(g, 2048)
        P_dma(fap.rearrange("p (fc kc f) -> p fc kc f", fc=2, kc=4),
              t_win[2 * g:2 * g + 2, ::2].rearrange(
                  "fc (kc p) f -> p fc kc f", p=P) if False else src_,
              [fb], reads=[fb]) if False else None
        for hh in range(2):
            fap2, fb2 = stD_ap(g * 2 + hh, 1024)
            P_dma(fap2.rearrange("p (kc f) -> p kc f", f=P),
                  t_win[2 * g + hh].rearrange("(kc p) f -> p kc f", p=P),
                  [fb2], reads=[fb2])
            P_act(aap_[:, hh * 1024:(hh + 1) * 1024], fap2, AF.Identity,
                  reads=[fb2], writes=[ab])
        P_dma(dst, aap_.rearrange("p (fc kc f) -> p fc kc f", fc=2, kc=8),
              [b_winbf], reads=[ab])
    for g in range(16):
        dst = t_w1q[2 * g:2 * g + 2].rearrange("fc (kc p) f -> p fc kc f", p=P)
        aap_, ab = stA_ap(g, 2048)
        for hh in range(2):
            fap2, fb2 = stD_ap(g * 2 + hh, 1024)
            P_dma(fap2.rearrange("p (kc f) -> p kc f", f=P),
                  t_w1t[2 * g + hh].rearrange("(kc p) f -> p kc f", p=P),
                  [fb2], reads=[fb2])
            P_ts(fap2, fap2, 64.0, RC, OP.mult, OP.add,
                 reads=[fb2], writes=[fb2])
            P_ts(fap2, fap2, RC, None, OP.subtract,
                 reads=[fb2], writes=[fb2])
            P_ts(aap_[:, hh * 1024:(hh + 1) * 1024], fap2, -128.0, 127.0,
                 OP.max, OP.min, reads=[fb2], writes=[ab])
        P_dma(dst, aap_.rearrange("p (fc kc f) -> p fc kc f", fc=2, kc=8),
              [b_w1q], reads=[ab])
    for g in range(32):
        fap2, fb2 = stD_ap(g, 1024)
        aap_, ab = stA_ap(g, 1024)
        P_dma(fap2, t_w2t[g * P:(g + 1) * P, :], [fb2], reads=[fb2])
        P_ts(fap2, fap2, 64.0, RC, OP.mult, OP.add, reads=[fb2], writes=[fb2])
        P_ts(fap2, fap2, RC, None, OP.subtract, reads=[fb2], writes=[fb2])
        P_ts(aap_, fap2, -128.0, 127.0, OP.max, OP.min,
             reads=[fb2], writes=[ab])
        P_dma(t_w2q[g * P:(g + 1) * P, :], aap_, [b_w2q], reads=[ab])
    for g in range(16):
        fap2, fb2 = stD_ap(g, 1024)
        aap_, ab = stA_ap(g, 1024)
        P_dma(fap2, t_wout[g * P:(g + 1) * P, :], [fb2], reads=[fb2])
        P_act(aap_, fap2, AF.Identity, reads=[fb2], writes=[ab])
        P_dma(t_woutbf[g * P:(g + 1) * P, :], aap_, [b_woutbf], reads=[ab])
    S.barrier()

    def ln_stats(src_ap, src_bufs, slot, mu_out, mu_buf, r_out, r_buf):
        o6 = stt[:, slot * 16: slot * 16 + 16]
        bs = b_stat[slot]
        S.op("DVE", lambda eng, s=src_ap: eng.bn_stats(o6[:, 0:6], s[:, 0:512]),
             reads=src_bufs, writes=[bs])
        S.op("DVE", lambda eng, s=src_ap: eng.bn_stats(o6[:, 6:12],
                                                       s[:, 512:1024]),
             reads=src_bufs, writes=[bs])
        S.op("DVE", lambda eng: eng.reduce_sum(o6[:, 12:13], o6[:, 1:12:3],
                                               axis=mybir.AxisListType.X),
             reads=[bs], writes=[bs])
        S.op("DVE", lambda eng: eng.reduce_sum(o6[:, 13:14], o6[:, 2:12:3],
                                               axis=mybir.AxisListType.X),
             reads=[bs], writes=[bs])
        S.op("DVE", lambda eng: eng.tensor_tensor(o6[:, 0:4], o6[:, 1:12:3],
                                                  o6[:, 1:12:3], OP.mult),
             reads=[bs], writes=[bs])
        S.op("DVE", lambda eng: eng.reduce_sum(o6[:, 14:15], o6[:, 0:4],
                                               axis=mybir.AxisListType.X),
             reads=[bs], writes=[bs])
        S.op("DVE", lambda eng: eng.scalar_tensor_tensor(
            out=o6[:, 14:15], in0=o6[:, 14:15], scalar=256.0, in1=o6[:, 13:14],
            op0=OP.mult, op1=OP.add),
            reads=[bs], writes=[bs])
        S.op("DVE", lambda eng: eng.tensor_single_scalar(
            out=o6[:, 14:15], in_=o6[:, 14:15], scalar=1.0 / 1024.0,
            op=OP.mult), reads=[bs], writes=[bs])
        S.op("DVE", lambda eng: eng.tensor_single_scalar(
            out=o6[:, 12:13], in_=o6[:, 12:13], scalar=0.25, op=OP.mult),
            reads=[bs], writes=[bs])
        S.op("DVE", lambda eng: eng.tensor_tensor(o6[:, 15:16], o6[:, 12:13],
                                                  o6[:, 12:13], OP.mult),
             reads=[bs], writes=[bs])
        S.op("DVE", lambda eng: eng.tensor_tensor(o6[:, 13:14], o6[:, 14:15],
                                                  o6[:, 15:16], OP.subtract),
             reads=[bs], writes=[bs])
        if mu_out is not None:
            P_copy(mu_out, o6[:, 12:13], reads=[bs], writes=[mu_buf])
        P_act(o6[:, 15:16], o6[:, 13:14], AF.Sqrt, bias=1e-5,
              reads=[bs], writes=[bs])
        S.op("DVE", lambda eng, o=r_out: eng.reciprocal(o, o6[:, 15:16]),
             reads=[bs], writes=[r_buf])

    def mean_of(slot):
        return stt[:, slot * 16 + 12: slot * 16 + 13], b_stat[slot]

    xib = lambda c: xi_sb[:, c * 1028 + 4: c * 1028 + 4 + 1024]

    for blk, (sq, tb) in enumerate([(0, 0), (0, 1), (1, 0), (1, 1)]):
        t0 = tb * TB
        if tb == 0:
            S.op("DVE", lambda eng: eng.memset(r_halo[:, :], 0.0),
                 writes=[b_halo])
        S.op("DVE", lambda eng: eng.tensor_copy(
            xi_sb[:, :].rearrange("p (c j) -> p c j", j=1028)[:, :, 1:4],
            r_halo[:, :].rearrange("p (c j) -> p c j", j=3)),
            reads=[b_halo], writes=list(b_xic), tag="halo_restore")

        def x_dma(tt, reg):
            P_dma(dA_(reg), t_x[sq, t0 + tt * P: t0 + (tt + 1) * P, :],
                  [b_xt[reg]])

        x_dma(0, 0)
        x_dma(1, 1)
        for tt in range(NT):
            reg = tt % 2
            ln_stats(dA_(reg), [b_xt[reg]], tt, mu1a[:, tt:tt + 1], b_mu1a,
                     r1a[:, tt:tt + 1], b_r1a)
            if tt + 2 < NT:
                x_dma(tt + 2, reg)
        x_dma(0, 0)
        x_dma(1, 1)
        for tt in range(NT):
            reg = tt % 2
            mu_ap, mu_buf = mean_of(tt)
            P_ts(dA_(2), dA_(reg), mu1a[:, tt:tt + 1], r1a[:, tt:tt + 1],
                 OP.subtract, OP.mult,
                 reads=[b_xt[reg], b_mu1a, b_r1a], writes=[b_xn2f])
            if tt + 2 < NT:
                x_dma(tt + 2, reg)
            ln_stats(dA_(2), [b_xn2f], tt, None, None, rrs[:, tt:tt + 1],
                     b_rrs[tt])
            P_ts(dA_(2), dA_(2), mu_ap, rrs[:, tt:tt + 1], OP.subtract,
                 OP.mult, reads=[b_xn2f, mu_buf, b_rrs[tt]], writes=[b_xn2f])
            for g in range(2):
                pb = psum[g]
                for j in range(4):
                    dc = g * 4 + j
                    P_tp(pb[:, j * P:(j + 1) * P], dA_(2)[:, dc * P:(dc + 1) * P],
                         reads=[b_xn2f], writes=[b_ps[g]])
                for j in range(4):
                    dc = g * 4 + j
                    P_act(xnfm[:, dc * 1024 + tt * P: dc * 1024 + (tt + 1) * P],
                          pb[:, j * P:(j + 1) * P], AF.Identity,
                          bias=r_lnb[:, dc:dc + 1], scale=r_lng[:, dc:dc + 1],
                          reads=[b_ps[g], b_lng, b_lnb], writes=[b_xnfm])

        def win_dma(fc):
            h = fc % 2
            P_dma(w_win[:, h * 1024:(h + 1) * 1024]
                  .rearrange("p (kc f) -> p kc f", f=P),
                  t_winbf[fc].rearrange("(kc p) f -> p kc f", p=P),
                  [b_wwin[h]], reads=[b_wwin[h]])

        def xz_mms(fc):
            h = fc % 2
            if fc + 1 < FC:
                win_dma(fc + 1)
            for th in range(2):
                pb = psum[2 + th]
                for kc in range(KCH):
                    P_mm(pb[:, :], w_win[:, h * 1024 + kc * P:
                                         h * 1024 + (kc + 1) * P],
                         xnfm[:, kc * 1024 + th * 512: kc * 1024 + (th + 1) * 512],
                         kc == 0, kc == KCH - 1,
                         reads=[b_wwin[h], b_xnfm], writes=[b_ps[2 + th]])

        win_dma(0)
        for c in range(DCH):
            xz_mms(c)
            for th in range(2):
                P_act(xi_sb[:, c * 1028 + 4 + th * 512:
                            c * 1028 + 4 + (th + 1) * 512],
                      psum[2 + th][:, :], AF.Identity,
                      reads=[b_ps[2 + th]], writes=[b_xic[c]])
            S.op("DVE", lambda eng, cc=c: eng.tensor_copy(
                r_halo[:, cc * 3:(cc + 1) * 3],
                xi_sb[:, cc * 1028 + 1025: cc * 1028 + 1028]),
                reads=[b_xic[c]], writes=[b_halo], tag="halo_save")
            for th in range(2):
                pb = psum[4 + th]
                for k in range(4):
                    for i4 in range(4):
                        S.op("PE", lambda eng,
                             o=pb[32 * i4:32 * i4 + 32, :],
                             l=r_cdiag[32 * i4:32 * i4 + 32,
                                       (c * 4 + k) * 32:(c * 4 + k + 1) * 32],
                             r=xi_sb[32 * i4:32 * i4 + 32,
                                     c * 1028 + 1 + k + th * 512:
                                     c * 1028 + 1 + k + th * 512 + 512],
                             st=(k == 0), sp=(k == 3), tp=(32 * i4, 32 * i4):
                             eng.matmul(o, l, r, start=st, stop=sp,
                                        tile_position=tp),
                             reads=[b_cdiag, b_xic[c], b_halo],
                             writes=[b_ps[4 + th]], tag="conv")
            for th in range(2):
                P_act(xi_sb[:, c * 1028 + 4 + th * 512:
                            c * 1028 + 4 + (th + 1) * 512],
                      psum[4 + th][:, :], AF.Silu, bias=r_cvb[:, c:c + 1],
                      reads=[b_ps[4 + th], b_cvb], writes=[b_xic[c]])
            for th in range(2):
                P_mm(psum[6 + th][0:33, :], r_wx[:, c * 33:(c + 1) * 33],
                     xi_sb[:, c * 1028 + 4 + th * 512:
                           c * 1028 + 4 + (th + 1) * 512],
                     c == 0, c == DCH - 1,
                     reads=[b_wx, b_xic[c]], writes=[b_ps[6 + th]])
        for th in range(2):
            P_act(dbc[0:33, th * 512:(th + 1) * 512], psum[6 + th][0:33, :],
                  AF.Identity, reads=[b_ps[6 + th]], writes=[b_dbc])
        for fc in range(DCH, FC):
            xz_mms(fc)
            for th in range(2):
                P_act(sz_sb[:, (fc - DCH) * 1024 + th * 512:
                            (fc - DCH) * 1024 + (th + 1) * 512],
                      psum[2 + th][:, :], AF.Silu,
                      reads=[b_ps[2 + th]], writes=[b_szc[fc - DCH]])

        def tap(nm, ap, bufs):
            if DEBUG_TAPS and blk == 0:
                S.dma("SP", lambda eng, o=t_dbg[nm], i=ap: eng.dma_start(
                    out=o, in_=i), reads=bufs, tag="tap" + nm)

        tap("dbg_xnfm", xnfm[:, :], [b_xnfm])
        tap("dbg_xi", xi_sb[:, :], list(b_xic))
        tap("dbg_sz", sz_sb[:, :], list(b_szc))
        tap("dbg_dbc", dbc[:, :], [b_dbc])
        S.alias(b_dA2 + b_dB2 + b_h2 + b_tmp2, [b_xnfm])
        S.alias(b_dlt, [b_slotB])
        S.alias(b_Brep + b_Crep, [b_x1])
        S.alias(b_spA + b_spB, [b_xn1, b_xn2f, b_faH[0], b_faH[1], b_ob])
        BrepAP = lambda r: slotC[:, (4 + r) * 1024:(5 + r) * 1024]
        CrepAP = lambda r: slotC[:, (6 + r) * 1024:(7 + r) * 1024]
        def em_bc(n):
            P_dma(bcB0[0:1, :], dbc[1 + n:2 + n, :], [b_bcB[0]],
                  reads=[b_dbc])
            P_dma(bcC0[0:1, :], dbc[17 + n:18 + n, :], [b_bcC[0]],
                  reads=[b_dbc])

        def em_rep(n):
            r = n % 2
            for th in range(2):
                P_mm(psum[6][:, :], r_ones[0:1, :],
                     bcB0[0:1, th * 512:(th + 1) * 512], True, True,
                     reads=[b_ones, b_bcB[0]], writes=[b_ps[6]])
                P_act(BrepAP(r)[:, th * 512:(th + 1) * 512], psum[6][:, :],
                      AF.Identity, reads=[b_ps[6]], writes=[b_Brep[r]])
                P_mm(psum[7][:, :], r_ones[0:1, :],
                     bcC0[0:1, th * 512:(th + 1) * 512], True, True,
                     reads=[b_ones, b_bcC[0]], writes=[b_ps[7]])
                P_act(CrepAP(r)[:, th * 512:(th + 1) * 512], psum[7][:, :],
                      AF.Identity, reads=[b_ps[7]], writes=[b_Crep[r]])

        for half in range(2):
            for dci in range(8):
                dc = half * 8 + dci
                for th in range(2):
                    P_mm(psum[6 + th][:, :], r_wdt[0:1, dc * P:(dc + 1) * P],
                         dbc[0:1, th * 512:(th + 1) * 512], True, True,
                         reads=[b_wdt, b_dbc], writes=[b_ps[6 + th]])
                aap = [dA_(1)[:, q * 512:(q + 1) * 512] for q in range(2)]
                map_ = [dA_(2)[:, q * 512:(q + 1) * 512] for q in range(2)]
                for th in range(2):
                    P_act(aap[th], psum[6 + th][:, :], AF.Relu,
                          bias=r_bdt[:, dc:dc + 1],
                          reads=[b_ps[6 + th], b_bdt], writes=[b_spA[th]])
                for th in range(2):
                    P_act(map_[th], psum[6 + th][:, :], AF.Abs,
                          bias=r_bdt[:, dc:dc + 1],
                          reads=[b_ps[6 + th], b_bdt], writes=[b_spB[th]])
                for th in range(2):
                    P_act(map_[th], map_[th], AF.Exp, scale=-1.0,
                          reads=[b_spB[th]], writes=[b_spB[th]])
                for th in range(2):
                    P_act(map_[th], map_[th], AF.Ln, bias=1.0,
                          reads=[b_spB[th]], writes=[b_spB[th]])
                for th in range(2):
                    P_tt(slotB[:, dci * 1024 + th * 512:
                               dci * 1024 + (th + 1) * 512],
                         aap[th], map_[th], OP.add,
                         reads=[b_spA[th], b_spB[th]], writes=[b_dlt[dci]],
                         tag="softplus")
            for dci in range(8):
                dc = half * 8 + dci
                P_tt(xib(dc), xib(dc),
                     slotB[:, dci * 1024:(dci + 1) * 1024], OP.mult,
                     reads=[b_xic[dc], b_dlt[dci]], writes=[b_xic[dc]], tag="u")
            if half == 0:
                tap("dbg_delta0", slotB[:, :], list(b_dlt))
                tap("dbg_u", xi_sb[:, :], list(b_xic))

            iters = [(n, dci) for n in range(NST) for dci in range(8)]
            NI = len(iters)

            def ap_ring(it):
                rg = it % 2
                return (rg,
                        xnfm[:, (rg * 3 + 0) * 1024:(rg * 3 + 1) * 1024],
                        xnfm[:, (rg * 3 + 1) * 1024:(rg * 3 + 2) * 1024],
                        xnfm[:, (rg * 3 + 2) * 1024:(rg * 3 + 3) * 1024])

            def em_dA(it):
                n, dci = iters[it]
                dc = half * 8 + dci
                rg, apA, apB, apH = ap_ring(it)
                P_act(apA, slotB[:, dci * 1024:(dci + 1) * 1024], AF.Exp,
                      scale=r_A[:, dc * NST + n: dc * NST + n + 1],
                      reads=[b_dlt[dci], b_A], writes=[b_dA2[rg]], tag="dAexp")

            def em_dB(it):
                n, dci = iters[it]
                dc = half * 8 + dci
                rg, apA, apB, apH = ap_ring(it)
                P_tt(apB, xib(dc), BrepAP(n % 2), OP.mult,
                     reads=[b_xic[dc], b_Brep[n % 2]], writes=[b_dB2[rg]],
                     tag="dB")

            def em_scan(it):
                n, dci = iters[it]
                dc = half * 8 + dci
                rg, apA, apB, apH = ap_ring(it)
                init = 0.0 if tb == 0 else r_carry[:, dc * NST + n:
                                                   dc * NST + n + 1]
                rd = [b_dA2[rg], b_dB2[rg]] + ([] if tb == 0 else [b_carry])
                S.op("DVE", lambda eng, i=init, a=apA, b=apB, hh=apH:
                     eng.tensor_tensor_scan(hh, a, b, i, OP.mult, OP.add),
                     reads=rd, writes=[b_h2[rg]], tag="scan")

            def em_hc(it, acc_now=True):
                n, dci = iters[it]
                dc = half * 8 + dci
                rg, apA, apB, apH = ap_ring(it)
                ya = slotA[:, dc * 1024:(dc + 1) * 1024]
                if n == 0:
                    P_tt(ya, apH, CrepAP(0), OP.mult,
                         reads=[b_h2[rg], b_Crep[0]], writes=[b_slotA])
                else:
                    tr = it % 2
                    apT = xnfm[:, (6 + tr) * 1024:(7 + tr) * 1024]
                    P_tt(apT, apH, CrepAP(n % 2), OP.mult,
                         reads=[b_h2[rg], b_Crep[n % 2]], writes=[b_tmp2[tr]])
                if tb == 0:
                    S.op("DVE", lambda eng, o=r_carry[:, dc * NST + n:
                                                      dc * NST + n + 1],
                         i=apH[:, 1023:1024]: eng.tensor_copy(o, i),
                         reads=[b_h2[rg]], writes=[b_carry], tag="carry")
                if acc_now:
                    em_acc(it)

            def em_acc(it):
                n, dci = iters[it]
                dc = half * 8 + dci
                if n == 0:
                    return
                tr = it % 2
                apT = xnfm[:, (6 + tr) * 1024:(7 + tr) * 1024]
                ya = slotA[:, dc * 1024:(dc + 1) * 1024]
                P_tt(ya, ya, apT, OP.add,
                     reads=[b_slotA, b_tmp2[tr]], writes=[b_slotA],
                     tag="yacc")

            em_bc(0)
            em_rep(0)
            for it in range(NI):
                n, dci = iters[it]
                if dci == 0 and n + 1 < NST:
                    em_bc(n + 1)
                if dci == 2 and n + 1 < NST:
                    em_rep(n + 1)
                em_dA(it)
                if it == 0:
                    em_dB(0)
                if it + 1 < NI:
                    em_dB(it + 1)
                em_scan(it)
                if it > 0:
                    em_hc(it - 1, acc_now=False)
                if it > 1:
                    em_acc(it - 2)
            em_hc(NI - 1, acc_now=False)
            em_acc(NI - 2)
            em_acc(NI - 1)
        tap("dbg_y", slotA[:, :], [b_slotA])
        for dc in range(DCH):
            S.op("POOL", lambda eng,
                 y=slotA[:, dc * 1024:(dc + 1) * 1024],
                 z=sz_sb[:, dc * 1024:(dc + 1) * 1024]:
                 eng.tensor_tensor(y, y, z, OP.mult),
                 reads=[b_slotA, b_szc[dc]], writes=[b_slotA], tag="gate")
        S.alias([b_slotB], b_dlt)
        S.alias([b_x1], b_Brep + b_Crep)
        S.alias([b_xnfm], b_dA2 + b_dB2 + b_h2 + b_tmp2)
        S.alias([b_faH[0], b_faH[1]], b_spA)
        S.alias([b_ob], b_spB)
        def wout_dma(kc2):
            h = kc2 % 4
            P_dma(w_wout[:, h * 1024:(h + 1) * 1024],
                  t_woutbf[(kc2 % DCH) * P:((kc2 % DCH) + 1) * P, :],
                  [b_wwout[h]], reads=[b_wwout[h]])

        def xb_dma(tt):
            P_dma(dA_(0), t_x[sq, t0 + tt * P: t0 + (tt + 1) * P, :],
                  [b_xt[0]])

        for pf in range(3):
            wout_dma(pf)
        for ttg in range(2):
            for kc in range(DCH):
                kc2 = ttg * DCH + kc
                if kc2 + 3 < 2 * DCH:
                    wout_dma(kc2 + 3)
                h = kc2 % 4
                for tti in range(4):
                    tt = ttg * 4 + tti
                    for fh in range(2):
                        pb = psum[tti * 2 + fh]
                        P_mm(pb[:, :],
                             slotA[:, kc * 1024 + tt * P: kc * 1024 + (tt + 1) * P],
                             w_wout[:, h * 1024 + fh * 512:
                                    h * 1024 + (fh + 1) * 512],
                             kc == 0, kc == DCH - 1,
                             reads=[b_slotA, b_wwout[h]],
                             writes=[b_ps[tti * 2 + fh]])
            xb_dma(ttg * 4)
            for tti in range(4):
                tt = ttg * 4 + tti
                P_ts(dA_(2), dA_(0), mu1a[:, tt:tt + 1], r1a[:, tt:tt + 1],
                     OP.subtract, OP.mult,
                     reads=[b_xt[0], b_mu1a, b_r1a], writes=[b_ob])
                for fh in range(2):
                    pb = psum[tti * 2 + fh]
                    P_tt(slotB[:, tt * 1024 + fh * 512:
                               tt * 1024 + (fh + 1) * 512],
                         dA_(2)[:, fh * 512:(fh + 1) * 512], pb[:, :], OP.add,
                         reads=[b_ob, b_ps[tti * 2 + fh]], writes=[b_slotB],
                         tag="t2")
                t2ap = slotB[:, tt * 1024:(tt + 1) * 1024]
                ln_stats(t2ap, [b_slotB], tt, None, None, rrs[:, tt:tt + 1],
                         b_rrs[tt])
                mu_ap, mu_buf = mean_of(tt)
                P_ts(dA_(2), t2ap, mu_ap, rrs[:, tt:tt + 1], OP.subtract,
                     OP.mult, reads=[b_slotB, mu_buf, b_rrs[tt]],
                     writes=[b_ob])
                P_tt(slotC[:, tt * 1024:(tt + 1) * 1024], dA_(2), dA_(0),
                     OP.add, reads=[b_ob, b_xt[0]], writes=[b_x1], tag="x1")
                if tti + 1 < 4:
                    xb_dma(ttg * 4 + tti + 1)
        tap("dbg_x1", slotC[:, :], [b_x1])
        for tt in range(NT):
            x1ap = slotC[:, tt * 1024:(tt + 1) * 1024]
            ln_stats(x1ap, [b_x1], tt, None, None, rrs[:, tt:tt + 1],
                     b_rrs[tt])
            r16 = stt[:, tt * 16 + 14: tt * 16 + 15]
            S.op("ACT", lambda eng, o=r16, i=rrs[:, tt:tt + 1]: eng.mul(
                o, i, 16.0), reads=[b_rrs[tt]], writes=[b_stat[tt]])
            mu_ap, mu_buf = mean_of(tt)
            P_ts(dA_(1), x1ap, mu_ap, r16, OP.subtract,
                 OP.mult, reads=[b_x1, mu_buf, b_stat[tt]],
                 writes=[b_faH[0], b_faH[1]])
            P_ts(dA_(1), dA_(1), RC, RC, OP.add, OP.subtract,
                 reads=[b_faH[0], b_faH[1]], writes=[b_faH[0], b_faH[1]])
            P_ts(dA_(1), dA_(1), -128.0, 127.0, OP.max, OP.min,
                 reads=[b_faH[0], b_faH[1]], writes=[b_faH[0], b_faH[1]])
            for g in range(2):
                pb = psum[g]
                for j in range(4):
                    dc = g * 4 + j
                    P_tp(pb[:, j * P:(j + 1) * P], dA_(1)[:, dc * P:(dc + 1) * P],
                         reads=[b_faH[0], b_faH[1]], writes=[b_ps[g]])
                for j in range(4):
                    dc = g * 4 + j
                    P_act(xnfm[:, dc * 1024 + tt * P: dc * 1024 + (tt + 1) * P],
                          pb[:, j * P:(j + 1) * P], AF.Identity,
                          reads=[b_ps[g]], writes=[b_xnfm])

        def w1_dma(hc2):
            h = hc2 % 4
            P_dma(w_w1[:, h * 1024:(h + 1) * 1024]
                  .rearrange("p (kc f) -> p kc f", f=P),
                  t_w1q[hc2 % HC].rearrange("(kc p) f -> p kc f", p=P),
                  [b_ww1[h]], reads=[b_ww1[h]])

        def w2_dma(kc2):
            h = kc2 % 4
            P_dma(w_w2[:, h * 1024:(h + 1) * 1024],
                  t_w2q[(kc2 % HC) * P:((kc2 % HC) + 1) * P, :],
                  [b_ww2[h]], reads=[b_ww2[h]])

        for pf in range(3):
            w1_dma(pf)
        for th in range(2):
            for hc in range(HC):
                hc2 = th * HC + hc
                if hc2 + 3 < 2 * HC:
                    w1_dma(hc2 + 3)
                h = hc2 % 4
                q = hc % 2
                pb = psum[2 + q]
                for kc in range(KCH):
                    P_mm(pb[:, :], w_w1[:, h * 1024 + kc * P:
                                        h * 1024 + (kc + 1) * P],
                         xnfm[:, kc * 1024 + th * 512: kc * 1024 + (th + 1) * 512],
                         kc == 0, kc == KCH - 1,
                         reads=[b_ww1[h], b_xnfm], writes=[b_ps[2 + q]])
                fap = dA_(1)[:, q * 512:(q + 1) * 512]
                P_act(fap, pb[:, :], AF.Identity,
                      scale=1.0 / 64.0, bias=r_b1q4[:, hc:hc + 1],
                      reads=[b_ps[2 + q], b_b1q4], writes=[b_faH[q]])
                P_ts(fap, fap, RC, RC, OP.add, OP.subtract,
                     reads=[b_faH[q]], writes=[b_faH[q]])
                P_ts(slotA[:, hc * 512:(hc + 1) * 512], fap,
                     0.0, 127.0, OP.max, OP.min, reads=[b_faH[q]],
                     writes=[b_slotA])
            for pf in range(3):
                w2_dma(th * HC + pf)
            for kc in range(HC):
                kc2 = th * HC + kc
                if kc2 + 3 < 2 * HC:
                    w2_dma(kc2 + 3)
                h = kc2 % 4
                for tti in range(4):
                    for fh in range(2):
                        pb = psum[tti * 2 + fh]
                        P_mm(pb[:, :],
                             slotA[:, kc * 512 + tti * P: kc * 512 + (tti + 1) * P],
                             w_w2[:, h * 1024 + fh * 512:
                                  h * 1024 + (fh + 1) * 512],
                             kc == 0, kc == HC - 1,
                             reads=[b_slotA, b_ww2[h]],
                             writes=[b_ps[tti * 2 + fh]])
            for tti in range(4):
                tt = th * 4 + tti
                for fh in range(2):
                    pb = psum[tti * 2 + fh]
                    S.op("DVE", lambda eng, o=dA_(1)[:, fh * 512:(fh + 1) * 512],
                         p=pb, f=fh: eng.scalar_tensor_tensor(
                             out=o, in0=p[:, :], scalar=1.0 / 1024.0,
                             in1=r_b2qrep[:, f * 512:(f + 1) * 512],
                             op0=OP.mult, op1=OP.add),
                         reads=[b_ps[tti * 2 + fh], b_b2qrep],
                         writes=[b_faH[fh]], tag="ffn2evac")
                ln_stats(dA_(1), [b_faH[0], b_faH[1]], tt, None, None,
                         rrs[:, tt:tt + 1], b_rrs[tt])
                mu_ap, mu_buf = mean_of(tt)
                P_ts(dA_(2), dA_(1), mu_ap, rrs[:, tt:tt + 1], OP.subtract,
                     OP.mult, reads=[b_faH[0], b_faH[1], mu_buf, b_rrs[tt]],
                     writes=[b_ob])
                P_tt(dA_(2), dA_(2), slotC[:, tt * 1024:(tt + 1) * 1024],
                     OP.add, reads=[b_ob, b_x1], writes=[b_ob], tag="out")
                S.dma("SP", lambda eng, a=dA_(2),
                      o=t_out[sq, t0 + tt * P: t0 + (tt + 1) * P, :]:
                      eng.dma_start(out=o, in_=a),
                      reads=[b_ob], writes=[], tag="outdma")
        S.alias([b_xn1], [b_faH[0], b_faH[1]])
        S.alias([b_xn2f], [b_ob])

    S.barrier()
    S.op("SP", lambda eng: eng.nop(), tag="final")
    S.emit(block)
    ex.close()
    return nc, "out"


def kernel(**inputs):
    shards = _stage(inputs)
    nc, out_name = build_program()
    res = run_bass_kernel_spmd(nc, shards, core_ids=list(range(8)))
    out = np.empty((16, L, D), np.float32)
    for c, r in enumerate(res.results):
        out[2 * c:2 * c + 2] = r[out_name]
    return out


# version 19
# speedup vs baseline: 1.0348x; 1.0348x over previous
import bisect
import contextlib
import numpy as np
import ml_dtypes

import concourse.bass as bass
import concourse.mybir as mybir
from concourse.bass_utils import run_bass_kernel_spmd

F32 = mybir.dt.float32
BF16 = mybir.dt.bfloat16
AF = mybir.ActivationFunctionType
OP = mybir.AluOpType
BFNP = ml_dtypes.bfloat16

RC = float(np.float32(1.5 * 2 ** 23))
CONSTS = (1e-5, 1.0 / 64.0, 16.0, 1.0 / 1024.0, -1.0, 1.0)

P = 128
D = 1024
DI = 2048
NST = 16
L = 2048
TB = 1024
NT = TB // P
DCH = DI // P
KCH = D // P
FC = 32
HC = 32

STREAMS = ("PE", "ACT", "DVE", "POOL", "SP")
DEBUG_TAPS = False


class Buf:
    __slots__ = ("name", "writer", "readers")

    def __init__(self, name):
        self.name = name
        self.writer = None
        self.readers = {}


class Op:
    __slots__ = ("fn", "waits", "inc_val", "tag", "dma", "dma_sem")

    def __init__(self, fn, tag, dma=False):
        self.fn = fn
        self.waits = []
        self.inc_val = None
        self.tag = tag
        self.dma = dma
        self.dma_sem = None


NDMASEM = 16


class Sched:
    def __init__(self, nc, sems):
        self.nc = nc
        self.sems = sems
        self.ops = {s: [] for s in STREAMS}
        self.incs = {s: [] for s in STREAMS}
        self.counts = {s: 0 for s in STREAMS}
        self.waited = {s: {} for s in STREAMS}
        self.dma_count = 0
        self.dma_uses = [0] * NDMASEM
        self.barrier_levels = {}

    def _resolve(self, src, op_idx):
        lst = self.incs[src]
        keys = [i for i, _ in lst]
        j = bisect.bisect_left(keys, op_idx)
        if j < len(lst):
            return lst[j][1]
        ops = self.ops[src]
        last = len(ops) - 1
        while last >= op_idx and ops[last].dma:
            last -= 1
        if last < op_idx:
            ops.append(Op(lambda eng: eng.nop(), "nop-inc"))
            last = len(ops) - 1
        self.counts[src] += 1
        v = self.counts[src]
        ops[last].inc_val = v
        lst.append((last, v))
        return v

    def _dep_val(self, dep):
        src, x = dep
        return (src, x if src == "DMA" else self._resolve(src, x))

    def op(self, stream, fn, reads=(), writes=(), is_dma=False, tag=""):
        deps = {}

        def add(dep, raw=False):
            if dep is None:
                return
            src, x = dep
            if src == stream and not raw:
                return
            if src not in deps or deps[src] < x:
                deps[src] = x

        for b in reads:
            add(b.writer, raw=True)
        for b in writes:
            add(b.writer)
            for src, x in b.readers.items():
                add((src, x))
        o = Op(fn, tag, dma=is_dma)
        idx = len(self.ops[stream])
        wl = self.waited[stream]

        def want(src, val):
            if wl.get(src, 0) >= val:
                return
            wl[src] = val
            o.waits.append((src, val))

        if is_dma:
            k = self.dma_count % NDMASEM
            self.dma_count += 1
            if self.dma_uses[k] > 0:
                add(("DMA%d" % k, 16 * self.dma_uses[k]))
            self.dma_uses[k] += 1
            o.dma_sem = "DMA%d" % k
        for src, x in deps.items():
            want(src, x if src.startswith("DMA") else self._resolve(src, x))
        for src, val in self.barrier_levels.items():
            if src != stream:
                want(src, val)
        self.ops[stream].append(o)
        if is_dma:
            dep = (o.dma_sem, 16 * self.dma_uses[(self.dma_count - 1) % NDMASEM])
        else:
            dep = (stream, idx)
        for b in writes:
            b.writer = dep
            b.readers = {}
        key, rv = dep if is_dma else (stream, idx)
        for b in reads:
            if b.readers.get(key, -1) < rv:
                b.readers[key] = rv
        return dep

    def dma(self, stream, fn, reads=(), writes=(), tag=""):
        return self.op(stream, fn, reads=reads, writes=writes, is_dma=True,
                       tag=tag)

    def alias(self, dst_bufs, src_bufs):
        for d in dst_bufs:
            for s in src_bufs:
                if s is d:
                    continue
                deps = list(s.readers.items())
                if s.writer is not None:
                    deps.append(s.writer)
                for src, x in deps:
                    if d.readers.get(src, -1) < x:
                        d.readers[src] = x

    def barrier(self):
        lv = {}
        for s in STREAMS:
            if self.ops[s]:
                lv[s] = self._resolve(s, len(self.ops[s]) - 1)
        for k in range(NDMASEM):
            if self.dma_uses[k]:
                lv["DMA%d" % k] = 16 * self.dma_uses[k]
        self.barrier_levels = lv

    def emit(self, block):
        engines = {
            "PE": block.tensor, "ACT": block.scalar, "DVE": block.vector,
            "POOL": block.gpsimd, "SP": block.sync,
        }
        nc = self.nc
        engmap = {
            "PE": nc.tensor, "ACT": nc.scalar, "DVE": nc.vector,
            "POOL": nc.gpsimd, "SP": nc.sync,
        }

        def make(stream):
            def body(_eng):
                eng = engmap[stream]
                sem_self = self.sems[stream]
                for o in self.ops[stream]:
                    for src, val in o.waits:
                        eng.wait_ge(self.sems[src], val)
                    inst = o.fn(eng)
                    if o.dma:
                        inst.then_inc(self.sems[o.dma_sem], 16)
                    if o.inc_val is not None:
                        assert inst is not None, (stream, o.tag)
                        inst.then_inc(sem_self, 1)
            return body

        for s in STREAMS:
            engines[s](make(s))



def _stage(inputs):
    f32 = np.float32
    x = np.ascontiguousarray(inputs["L0_attn_x"], dtype=f32)
    W_in = np.asarray(inputs["L0_attn_W_in"], f32)
    conv_w = np.asarray(inputs["L0_attn_conv_w"], f32)
    conv_b = np.asarray(inputs["L0_attn_conv_b"], f32)
    W_x = np.asarray(inputs["L0_attn_W_x"], f32)
    W_dt = np.asarray(inputs["L0_attn_W_dt"], f32)
    b_dt = np.asarray(inputs["L0_attn_b_dt"], f32)
    W_out = np.asarray(inputs["L0_attn_W_out"], f32)
    log_A = np.asarray(inputs["L0_attn_log_A"], f32)
    ln_g = np.asarray(inputs["L0_attn_ln_g"], f32)
    ln_b = np.asarray(inputs["L0_attn_ln_b"], f32)
    w1 = np.asarray(inputs["L0_ffn_w1"], f32)
    b1 = np.asarray(inputs["L0_ffn_b1"], f32)
    w2 = np.asarray(inputs["L0_ffn_w2"], f32)
    b2 = np.asarray(inputs["L0_ffn_b2"], f32)

    win_s = np.ascontiguousarray(
        W_in.reshape(D, FC, P).transpose(1, 0, 2))
    w1t_s = np.ascontiguousarray(
        w1.reshape(HC, P, D).transpose(0, 2, 1))
    w2t = np.ascontiguousarray(w2.T)

    convdiag = np.zeros((P, 64 * 32), f32)
    j = np.arange(P)
    for dc in range(DCH):
        for k in range(4):
            convdiag[j, (dc * 4 + k) * 32 + (j % 32)] = conv_w[dc * P + j, 0, k]

    def tcol(v, n):
        return np.ascontiguousarray(v.reshape(n, P).T)

    loga_t = np.ascontiguousarray(
        log_A.reshape(DCH, P, NST).transpose(1, 0, 2).reshape(P, DCH * NST))

    common = dict(
        win=win_s, wx=np.ascontiguousarray(W_x), wdt=np.ascontiguousarray(W_dt),
        wout=np.ascontiguousarray(W_out), w1t=w1t_s, w2t=w2t,
        convdiag=convdiag, cvb=tcol(conv_b, DCH), bdt=tcol(b_dt, DCH),
        loga=loga_t, lng=tcol(ln_g, KCH), lnb=tcol(ln_b, KCH),
        b1=tcol(b1, HC), b2=np.ascontiguousarray(b2[None, :]),
        eye=np.eye(P, dtype=f32),
    )
    shards = []
    for c in range(8):
        m = dict(common)
        m["x"] = np.ascontiguousarray(x[2 * c:2 * c + 2])
        shards.append(m)
    return shards



def build_program():
    nc = bass.Bass("TRN2", target_bir_lowering=False)
    ex = contextlib.ExitStack()

    def din(name, shape, dt=F32, kind="ExternalInput"):
        return nc.dram_tensor(name, shape, dt, kind=kind).ap()

    t_x = din("x", [2, L, D])
    t_win = din("win", [FC, D, P])
    t_wx = din("wx", [DI, 33])
    t_wdt = din("wdt", [1, DI])
    t_wout = din("wout", [DI, D])
    t_w1t = din("w1t", [HC, D, P])
    t_w2t = din("w2t", [4 * D, D])
    t_cdiag = din("convdiag", [P, 64 * 32])
    t_cvb = din("cvb", [P, DCH])
    t_bdt = din("bdt", [P, DCH])
    t_loga = din("loga", [P, DCH * NST])
    t_lng = din("lng", [P, KCH])
    t_lnb = din("lnb", [P, KCH])
    t_b1 = din("b1", [P, HC])
    t_b2 = din("b2", [1, D])
    t_eye = din("eye", [P, P])
    t_out = din("out", [2, L, D], kind="ExternalOutput")
    t_dbg = {}
    if DEBUG_TAPS:
        for nm, shp, dt in [
            ("dbg_xnfm", [P, KCH * 1024], BF16),
            ("dbg_xi", [P, DCH * 1028], BF16),
            ("dbg_sz", [P, DCH * 1024], BF16),
            ("dbg_dbc", [P, 1024], BF16),
            ("dbg_delta0", [P, 8 * 1024], BF16),
            ("dbg_u", [P, DCH * 1028], BF16),
            ("dbg_y", [P, DCH * 1024], BF16),
            ("dbg_x1", [P, 8 * 1024], BF16),
        ]:
            t_dbg[nm] = din(nm, shp, dt, kind="ExternalOutput")
    t_winbf = din("winbf", [FC, D, P], BF16, kind="Internal")
    t_w1q = din("w1q", [HC, D, P], BF16, kind="Internal")
    t_w2q = din("w2q", [4 * D, D], BF16, kind="Internal")
    t_woutbf = din("woutbf", [DI, D], BF16, kind="Internal")

    def sb(name, shape, dt):
        return ex.enter_context(nc.sbuf_tensor(name, shape, dt))

    r_wx = sb("r_wx", [P, DCH * 33], BF16)
    r_cdiag = sb("r_cdiag", [P, 64 * 32], BF16)
    r_A = sb("r_A", [P, DCH * NST], F32)
    r_wdt = sb("r_wdt", [1, DI], BF16)
    r_cvb = sb("r_cvb", [P, DCH], F32)
    r_bdt = sb("r_bdt", [P, DCH], F32)
    r_lng = sb("r_lng", [P, KCH], F32)
    r_lnb = sb("r_lnb", [P, KCH], F32)
    r_b1q4 = sb("r_b1q4", [P, HC], F32)
    r_b2qrep = sb("r_b2qrep", [P, D], BF16)
    r_eye = sb("r_eye", [P, P], F32)
    r_consts = sb("r_consts", [P, 8], F32)
    r_ones = sb("r_ones", [1, P], BF16)
    r_carry = sb("r_carry", [P, DCH * NST], F32)
    r_halo = sb("r_halo", [P, DCH * 3], BF16)
    w_win = sb("w_win", [P, 2 * KCH * P], BF16)
    w_w1 = sb("w_w1", [P, 4 * KCH * P], BF16)
    w_w2 = sb("w_w2", [P, 4 * D], BF16)
    w_wout = sb("w_wout", [P, 4 * D], BF16)
    xi_sb = sb("xi_sb", [P, DCH * 1028], BF16)
    sz_sb = sb("sz_sb", [P, DCH * 1024], BF16)
    xnfm = sb("xnfm", [P, KCH * 1024], BF16)
    dbc = sb("dbc", [P, 1024], BF16)
    bcB0 = sb("bcB0", [1, 1024], BF16)
    bcC0 = sb("bcC0", [1, 1024], BF16)
    slotA = sb("slotA", [P, 16 * 1024], BF16)
    slotB = sb("slotB", [P, 8 * 1024], BF16)
    slotC = sb("slotC", [P, 8 * 1024], BF16)
    slotD = sb("slotD", [P, 3072], F32)
    stt = sb("stt", [P, 8 * 16], F32)
    rrs = sb("rrs", [P, 8], F32)
    mu1a = sb("mu1a", [P, NT], F32)
    r1a = sb("r1a", [P, NT], F32)
    psum = [ex.enter_context(nc.psum_tensor(f"ps{i}", [P, 512], F32))
            for i in range(8)]

    sems = {}
    for s in list(STREAMS) + ["DMA%d" % i for i in range(NDMASEM)]:
        sems[s] = ex.enter_context(nc.semaphore("sem_" + s))
    block = ex.enter_context(nc.Block())
    S = Sched(nc, sems)

    B = lambda n: Buf(n)
    b_wx, b_cdiag, b_A, b_wdt = B("wx"), B("cdiag"), B("A"), B("wdt")
    b_cvb, b_bdt, b_lng, b_lnb = B("cvb"), B("bdt"), B("lng"), B("lnb")
    b_b1q4, b_b2qrep, b_eye, b_ones = B("b1q4"), B("b2qrep"), B("eye"), B("ones")
    b_carry, b_halo = B("carry"), B("halo")
    b_wwin = [B("wwin0"), B("wwin1")]
    b_ww1 = [B(f"ww1{i}") for i in range(4)]
    b_ww2 = [B(f"ww2{i}") for i in range(4)]
    b_wwout = [B(f"wwout{i}") for i in range(4)]
    b_xic = [B(f"xi{i}") for i in range(DCH)]
    b_szc = [B(f"sz{i}") for i in range(DCH)]
    b_xnfm, b_dbc = B("xnfm"), B("dbc")
    b_bcB = [B("bcB0")]
    b_bcC = [B("bcC0")]
    b_slotA, b_slotB = B("slotA"), B("slotB")
    b_dlt = [B(f"dlt{i}") for i in range(8)]
    b_xn2f = B("xn2f")
    b_dA2 = [B("dA0"), B("dA1")]
    b_dB2 = [B("dB0"), B("dB1")]
    b_h2 = [B("h0"), B("h1")]
    b_tmp2 = [B("tmp0"), B("tmp1")]
    b_Brep = [B("Brep0"), B("Brep1")]
    b_Crep = [B("Crep0"), B("Crep1")]
    b_x1 = B("x1")
    b_xt = [B("xt0"), B("xt1")]
    b_xn1 = B("xn1")
    b_faH = [B("fa0"), B("fa1")]
    b_ob = B("ob")
    b_spA = [B("spA0"), B("spA1")]
    b_spB = [B("spB0"), B("spB1")]
    b_stat = [B(f"stat{i}") for i in range(8)]
    b_rrs = [B(f"rrs{i}") for i in range(8)]
    b_mu1a, b_r1a = B("mu1a"), B("r1a")
    b_ps = [B(f"ps{i}") for i in range(8)]
    b_winbf, b_w1q, b_w2q, b_woutbf = B("winbf"), B("w1q"), B("w2q"), B("woutbf")
    b_stD = [B(f"stD{i}") for i in range(3)]
    b_stA = [B(f"stA{i}") for i in range(4)]

    cA = lambda i: slotC[:, i * 1024:(i + 1) * 1024]
    dA_ = lambda i: slotD[:, i * 1024:(i + 1) * 1024]

    def P_dma(dst, src, writes, reads=(), tag=""):
        S.dma("SP", lambda eng, d=dst, s_=src: eng.dma_start(out=d, in_=s_),
              reads=reads, writes=writes, tag=tag)

    def P_act(out, in_, func, bias=0.0, scale=1.0, reads=(), writes=(), tag=""):
        S.op("ACT", lambda eng, o=out, i=in_, f=func, b=bias, sc=scale:
             eng.activation(out=o, in_=i, func=f, bias=b, scale=sc),
             reads=reads, writes=writes, tag=tag)

    def P_ts(out, in0, s1, s2, op0, op1=None, stream="DVE", reads=(), writes=(),
             tag=""):
        if s2 is None:
            S.op(stream, lambda eng, o=out, i=in0, a=s1, x=op0:
                 eng.tensor_single_scalar(out=o, in_=i, scalar=a, op=x),
                 reads=reads, writes=writes, tag=tag)
        else:
            S.op(stream, lambda eng, o=out, i=in0, a=s1, b=s2, x=op0, y=op1:
                 eng.tensor_scalar(out=o, in0=i, scalar1=a, scalar2=b, op0=x,
                                   op1=y),
                 reads=reads, writes=writes, tag=tag)

    def P_tt(out, in0, in1, op, reads=(), writes=(), tag=""):
        S.op("DVE", lambda eng, o=out, a=in0, b=in1, x=op:
             eng.tensor_tensor(o, a, b, x), reads=reads, writes=writes, tag=tag)

    def P_copy(out, in_, stream="DVE", reads=(), writes=(), tag=""):
        S.op(stream, lambda eng, o=out, i=in_: eng.tensor_copy(o, i),
             reads=reads, writes=writes, tag=tag)

    def P_mm(out, lhsT, rhs, start, stop, reads=(), writes=(), tag=""):
        S.op("PE", lambda eng, o=out, l=lhsT, r=rhs, st=start, sp=stop:
             eng.matmul(o, l, r, start=st, stop=sp),
             reads=reads, writes=writes, tag=tag)

    def P_tp(out, in_, reads=(), writes=(), tag=""):
        S.op("PE", lambda eng, o=out, i=in_: eng.transpose(o, i, r_eye[:, :]),
             reads=list(reads) + [b_eye], writes=writes, tag=tag)

    b_consts = B("consts")
    for ci, cv in enumerate(CONSTS):
        S.op("DVE", lambda eng, i=ci, v=cv: eng.memset(r_consts[:, i:i + 1], v),
             writes=[b_consts])
        nc.const_aps.register(F32, cv, r_consts[:, ci:ci + 1])
    P_dma(r_eye[:, :], t_eye, [b_eye])
    S.op("DVE", lambda eng: eng.memset(r_ones[:, :], 1.0), writes=[b_ones])
    P_dma(dA_(0)[:, :DCH * NST], t_loga, [b_stD[0]], reads=[b_stD[0]])
    P_act(dA_(1)[:, :DCH * NST], dA_(0)[:, :DCH * NST], AF.Exp,
          reads=[b_stD[0]], writes=[b_stD[0]])
    P_ts(r_A[:, :], dA_(1)[:, :DCH * NST], -1.0, None, OP.mult,
         reads=[b_stD[0]], writes=[b_A])
    P_dma(r_cvb[:, :], t_cvb, [b_cvb])
    P_dma(r_bdt[:, :], t_bdt, [b_bdt])
    P_dma(r_lng[:, :], t_lng, [b_lng])
    P_dma(r_lnb[:, :], t_lnb, [b_lnb])
    P_dma(dA_(0)[:, :HC], t_b1, [b_stD[0]], reads=[b_stD[0]])
    P_ts(dA_(1)[:, :HC], dA_(0)[:, :HC], 64.0, RC, OP.mult, OP.add,
         reads=[b_stD[0]], writes=[b_stD[0]])
    P_ts(dA_(1)[:, :HC], dA_(1)[:, :HC], RC, 0.25, OP.subtract, OP.mult,
         reads=[b_stD[0]], writes=[b_stD[0]])
    P_ts(r_b1q4[:, :], dA_(1)[:, :HC], -32.0, 31.75, OP.max, OP.min,
         reads=[b_stD[0]], writes=[b_b1q4])
    P_dma(dA_(0)[0:1, :D], t_b2, [b_stD[0]], reads=[b_stD[0]])
    P_ts(dA_(1)[0:1, :D], dA_(0)[0:1, :D], 64.0, RC, OP.mult, OP.add,
         reads=[b_stD[0]], writes=[b_stD[0]])
    P_ts(dA_(1)[0:1, :D], dA_(1)[0:1, :D], RC, 1.0 / 64.0, OP.subtract, OP.mult,
         reads=[b_stD[0]], writes=[b_stD[0]])
    P_ts(r_b2qrep[0:1, :], dA_(1)[0:1, :D], -2.0, 127.0 / 64.0, OP.max, OP.min,
         reads=[b_stD[0]], writes=[b_b2qrep])
    pp = 1
    while pp < P:
        P_dma(r_b2qrep[pp:min(2 * pp, P), :], r_b2qrep[0:min(pp, P - pp), :],
              [b_b2qrep], reads=[b_b2qrep])
        pp *= 2
    P_dma(dA_(0)[:, :DCH * 33].rearrange("p (kc j) -> p kc j", j=33),
          t_wx.rearrange("(kc p) j -> p kc j", p=P), [b_stD[0]], reads=[b_stD[0]])
    P_act(r_wx[:, :], dA_(0)[:, :DCH * 33], AF.Identity,
          reads=[b_stD[0]], writes=[b_wx])
    P_dma(slotD[0:1, :DI], t_wdt, [b_stD[0]], reads=[b_stD[0]])
    P_act(r_wdt[:, :], slotD[0:1, :DI], AF.Identity,
          reads=[b_stD[0]], writes=[b_wdt])
    P_dma(slotD[:, :2048], t_cdiag[:, :], [b_stD[0]], reads=[b_stD[0]])
    P_act(r_cdiag[:, :], slotD[:, :2048], AF.Identity,
          reads=[b_stD[0]], writes=[b_cdiag])
    def stD_ap(i, n):
        r = i % 3
        return slotD[:, r * 1024: r * 1024 + n], b_stD[r]

    def stA_ap(i, n):
        r = i % 4
        return slotA[:, r * 2048: r * 2048 + n], b_stA[r]

    for g in range(16):
        src_ = t_win[2 * g:2 * g + 2].rearrange("fc (kc p) f -> p fc kc f", p=P)
        dst = t_winbf[2 * g:2 * g + 2].rearrange("fc (kc p) f -> p fc kc f", p=P)
        fap, fb = stD_ap(g, 1024)
        aap_, ab = stA_ap(g, 2048)
        P_dma(fap.rearrange("p (fc kc f) -> p fc kc f", fc=2, kc=4),
              t_win[2 * g:2 * g + 2, ::2].rearrange(
                  "fc (kc p) f -> p fc kc f", p=P) if False else src_,
              [fb], reads=[fb]) if False else None
        for hh in range(2):
            fap2, fb2 = stD_ap(g * 2 + hh, 1024)
            P_dma(fap2.rearrange("p (kc f) -> p kc f", f=P),
                  t_win[2 * g + hh].rearrange("(kc p) f -> p kc f", p=P),
                  [fb2], reads=[fb2])
            P_act(aap_[:, hh * 1024:(hh + 1) * 1024], fap2, AF.Identity,
                  reads=[fb2], writes=[ab])
        P_dma(dst, aap_.rearrange("p (fc kc f) -> p fc kc f", fc=2, kc=8),
              [b_winbf], reads=[ab])
    for g in range(16):
        dst = t_w1q[2 * g:2 * g + 2].rearrange("fc (kc p) f -> p fc kc f", p=P)
        aap_, ab = stA_ap(g, 2048)
        for hh in range(2):
            fap2, fb2 = stD_ap(g * 2 + hh, 1024)
            P_dma(fap2.rearrange("p (kc f) -> p kc f", f=P),
                  t_w1t[2 * g + hh].rearrange("(kc p) f -> p kc f", p=P),
                  [fb2], reads=[fb2])
            P_ts(fap2, fap2, 64.0, RC, OP.mult, OP.add,
                 reads=[fb2], writes=[fb2])
            P_ts(fap2, fap2, RC, None, OP.subtract,
                 reads=[fb2], writes=[fb2])
            P_ts(aap_[:, hh * 1024:(hh + 1) * 1024], fap2, -128.0, 127.0,
                 OP.max, OP.min, reads=[fb2], writes=[ab])
        P_dma(dst, aap_.rearrange("p (fc kc f) -> p fc kc f", fc=2, kc=8),
              [b_w1q], reads=[ab])
    for g in range(32):
        fap2, fb2 = stD_ap(g, 1024)
        aap_, ab = stA_ap(g, 1024)
        P_dma(fap2, t_w2t[g * P:(g + 1) * P, :], [fb2], reads=[fb2])
        P_ts(fap2, fap2, 64.0, RC, OP.mult, OP.add, reads=[fb2], writes=[fb2])
        P_ts(fap2, fap2, RC, None, OP.subtract, reads=[fb2], writes=[fb2])
        P_ts(aap_, fap2, -128.0, 127.0, OP.max, OP.min,
             reads=[fb2], writes=[ab])
        P_dma(t_w2q[g * P:(g + 1) * P, :], aap_, [b_w2q], reads=[ab])
    for g in range(16):
        fap2, fb2 = stD_ap(g, 1024)
        aap_, ab = stA_ap(g, 1024)
        P_dma(fap2, t_wout[g * P:(g + 1) * P, :], [fb2], reads=[fb2])
        P_act(aap_, fap2, AF.Identity, reads=[fb2], writes=[ab])
        P_dma(t_woutbf[g * P:(g + 1) * P, :], aap_, [b_woutbf], reads=[ab])
    S.barrier()

    def ln_stats(src_ap, src_bufs, slot, mu_out, mu_buf, r_out, r_buf):
        o6 = stt[:, slot * 16: slot * 16 + 16]
        bs = b_stat[slot]
        S.op("DVE", lambda eng, s=src_ap: eng.bn_stats(o6[:, 0:6], s[:, 0:512]),
             reads=src_bufs, writes=[bs])
        S.op("DVE", lambda eng, s=src_ap: eng.bn_stats(o6[:, 6:12],
                                                       s[:, 512:1024]),
             reads=src_bufs, writes=[bs])
        S.op("DVE", lambda eng: eng.bn_aggr(o6[:, 12:14], o6[:, 0:12]),
             reads=[bs], writes=[bs])
        if mu_out is not None:
            P_copy(mu_out, o6[:, 12:13], reads=[bs], writes=[mu_buf])
        P_act(o6[:, 15:16], o6[:, 13:14], AF.Sqrt, bias=1e-5,
              reads=[bs], writes=[bs])
        S.op("DVE", lambda eng, o=r_out: eng.reciprocal(o, o6[:, 15:16]),
             reads=[bs], writes=[r_buf])

    def mean_of(slot):
        return stt[:, slot * 16 + 12: slot * 16 + 13], b_stat[slot]

    xib = lambda c: xi_sb[:, c * 1028 + 4: c * 1028 + 4 + 1024]

    for blk, (sq, tb) in enumerate([(0, 0), (0, 1), (1, 0), (1, 1)]):
        t0 = tb * TB
        if tb == 0:
            S.op("DVE", lambda eng: eng.memset(r_halo[:, :], 0.0),
                 writes=[b_halo])
        S.op("DVE", lambda eng: eng.tensor_copy(
            xi_sb[:, :].rearrange("p (c j) -> p c j", j=1028)[:, :, 1:4],
            r_halo[:, :].rearrange("p (c j) -> p c j", j=3)),
            reads=[b_halo], writes=list(b_xic), tag="halo_restore")

        def x_dma(tt, reg):
            P_dma(dA_(reg), t_x[sq, t0 + tt * P: t0 + (tt + 1) * P, :],
                  [b_xt[reg]])

        x_dma(0, 0)
        x_dma(1, 1)
        for tt in range(NT):
            reg = tt % 2
            ln_stats(dA_(reg), [b_xt[reg]], tt, mu1a[:, tt:tt + 1], b_mu1a,
                     r1a[:, tt:tt + 1], b_r1a)
            if tt + 2 < NT:
                x_dma(tt + 2, reg)
        x_dma(0, 0)
        x_dma(1, 1)
        for tt in range(NT):
            reg = tt % 2
            mu_ap, mu_buf = mean_of(tt)
            P_ts(dA_(2), dA_(reg), mu1a[:, tt:tt + 1], r1a[:, tt:tt + 1],
                 OP.subtract, OP.mult,
                 reads=[b_xt[reg], b_mu1a, b_r1a], writes=[b_xn2f])
            if tt + 2 < NT:
                x_dma(tt + 2, reg)
            ln_stats(dA_(2), [b_xn2f], tt, None, None, rrs[:, tt:tt + 1],
                     b_rrs[tt])
            P_ts(dA_(2), dA_(2), mu_ap, rrs[:, tt:tt + 1], OP.subtract,
                 OP.mult, reads=[b_xn2f, mu_buf, b_rrs[tt]], writes=[b_xn2f])
            for g in range(2):
                pb = psum[g]
                for j in range(4):
                    dc = g * 4 + j
                    P_tp(pb[:, j * P:(j + 1) * P], dA_(2)[:, dc * P:(dc + 1) * P],
                         reads=[b_xn2f], writes=[b_ps[g]])
                for j in range(4):
                    dc = g * 4 + j
                    P_act(xnfm[:, dc * 1024 + tt * P: dc * 1024 + (tt + 1) * P],
                          pb[:, j * P:(j + 1) * P], AF.Identity,
                          bias=r_lnb[:, dc:dc + 1], scale=r_lng[:, dc:dc + 1],
                          reads=[b_ps[g], b_lng, b_lnb], writes=[b_xnfm])

        def win_dma(fc):
            h = fc % 2
            P_dma(w_win[:, h * 1024:(h + 1) * 1024]
                  .rearrange("p (kc f) -> p kc f", f=P),
                  t_winbf[fc].rearrange("(kc p) f -> p kc f", p=P),
                  [b_wwin[h]], reads=[b_wwin[h]])

        def xz_mms(fc):
            h = fc % 2
            if fc + 1 < FC:
                win_dma(fc + 1)
            for th in range(2):
                pb = psum[2 + th]
                for kc in range(KCH):
                    P_mm(pb[:, :], w_win[:, h * 1024 + kc * P:
                                         h * 1024 + (kc + 1) * P],
                         xnfm[:, kc * 1024 + th * 512: kc * 1024 + (th + 1) * 512],
                         kc == 0, kc == KCH - 1,
                         reads=[b_wwin[h], b_xnfm], writes=[b_ps[2 + th]])

        win_dma(0)
        for c in range(DCH):
            xz_mms(c)
            for th in range(2):
                P_act(xi_sb[:, c * 1028 + 4 + th * 512:
                            c * 1028 + 4 + (th + 1) * 512],
                      psum[2 + th][:, :], AF.Identity,
                      reads=[b_ps[2 + th]], writes=[b_xic[c]])
            S.op("DVE", lambda eng, cc=c: eng.tensor_copy(
                r_halo[:, cc * 3:(cc + 1) * 3],
                xi_sb[:, cc * 1028 + 1025: cc * 1028 + 1028]),
                reads=[b_xic[c]], writes=[b_halo], tag="halo_save")
            for th in range(2):
                pb = psum[4 + th]
                for k in range(4):
                    for i4 in range(4):
                        S.op("PE", lambda eng,
                             o=pb[32 * i4:32 * i4 + 32, :],
                             l=r_cdiag[32 * i4:32 * i4 + 32,
                                       (c * 4 + k) * 32:(c * 4 + k + 1) * 32],
                             r=xi_sb[32 * i4:32 * i4 + 32,
                                     c * 1028 + 1 + k + th * 512:
                                     c * 1028 + 1 + k + th * 512 + 512],
                             st=(k == 0), sp=(k == 3), tp=(32 * i4, 32 * i4):
                             eng.matmul(o, l, r, start=st, stop=sp,
                                        tile_position=tp),
                             reads=[b_cdiag, b_xic[c], b_halo],
                             writes=[b_ps[4 + th]], tag="conv")
            for th in range(2):
                P_act(xi_sb[:, c * 1028 + 4 + th * 512:
                            c * 1028 + 4 + (th + 1) * 512],
                      psum[4 + th][:, :], AF.Silu, bias=r_cvb[:, c:c + 1],
                      reads=[b_ps[4 + th], b_cvb], writes=[b_xic[c]])
            for th in range(2):
                P_mm(psum[6 + th][0:33, :], r_wx[:, c * 33:(c + 1) * 33],
                     xi_sb[:, c * 1028 + 4 + th * 512:
                           c * 1028 + 4 + (th + 1) * 512],
                     c == 0, c == DCH - 1,
                     reads=[b_wx, b_xic[c]], writes=[b_ps[6 + th]])
        for th in range(2):
            P_act(dbc[0:33, th * 512:(th + 1) * 512], psum[6 + th][0:33, :],
                  AF.Identity, reads=[b_ps[6 + th]], writes=[b_dbc])
        for fc in range(DCH, FC):
            xz_mms(fc)
            for th in range(2):
                P_act(sz_sb[:, (fc - DCH) * 1024 + th * 512:
                            (fc - DCH) * 1024 + (th + 1) * 512],
                      psum[2 + th][:, :], AF.Silu,
                      reads=[b_ps[2 + th]], writes=[b_szc[fc - DCH]])

        def tap(nm, ap, bufs):
            if DEBUG_TAPS and blk == 0:
                S.dma("SP", lambda eng, o=t_dbg[nm], i=ap: eng.dma_start(
                    out=o, in_=i), reads=bufs, tag="tap" + nm)

        tap("dbg_xnfm", xnfm[:, :], [b_xnfm])
        tap("dbg_xi", xi_sb[:, :], list(b_xic))
        tap("dbg_sz", sz_sb[:, :], list(b_szc))
        tap("dbg_dbc", dbc[:, :], [b_dbc])
        S.alias(b_dA2 + b_dB2 + b_h2 + b_tmp2, [b_xnfm])
        S.alias(b_dlt, [b_slotB])
        S.alias(b_Brep + b_Crep, [b_x1])
        S.alias(b_spA + b_spB, [b_xn1, b_xn2f, b_faH[0], b_faH[1], b_ob])
        BrepAP = lambda r: slotC[:, (4 + r) * 1024:(5 + r) * 1024]
        CrepAP = lambda r: slotC[:, (6 + r) * 1024:(7 + r) * 1024]
        def em_bc(n):
            P_dma(bcB0[0:1, :], dbc[1 + n:2 + n, :], [b_bcB[0]],
                  reads=[b_dbc])
            P_dma(bcC0[0:1, :], dbc[17 + n:18 + n, :], [b_bcC[0]],
                  reads=[b_dbc])

        def em_rep(n):
            r = n % 2
            for th in range(2):
                P_mm(psum[6][:, :], r_ones[0:1, :],
                     bcB0[0:1, th * 512:(th + 1) * 512], True, True,
                     reads=[b_ones, b_bcB[0]], writes=[b_ps[6]])
                P_act(BrepAP(r)[:, th * 512:(th + 1) * 512], psum[6][:, :],
                      AF.Identity, reads=[b_ps[6]], writes=[b_Brep[r]])
                P_mm(psum[7][:, :], r_ones[0:1, :],
                     bcC0[0:1, th * 512:(th + 1) * 512], True, True,
                     reads=[b_ones, b_bcC[0]], writes=[b_ps[7]])
                P_act(CrepAP(r)[:, th * 512:(th + 1) * 512], psum[7][:, :],
                      AF.Identity, reads=[b_ps[7]], writes=[b_Crep[r]])

        for half in range(2):
            for dci in range(8):
                dc = half * 8 + dci
                for th in range(2):
                    P_mm(psum[6 + th][:, :], r_wdt[0:1, dc * P:(dc + 1) * P],
                         dbc[0:1, th * 512:(th + 1) * 512], True, True,
                         reads=[b_wdt, b_dbc], writes=[b_ps[6 + th]])
                aap = [dA_(1)[:, q * 512:(q + 1) * 512] for q in range(2)]
                map_ = [dA_(2)[:, q * 512:(q + 1) * 512] for q in range(2)]
                for th in range(2):
                    P_act(aap[th], psum[6 + th][:, :], AF.Relu,
                          bias=r_bdt[:, dc:dc + 1],
                          reads=[b_ps[6 + th], b_bdt], writes=[b_spA[th]])
                for th in range(2):
                    P_act(map_[th], psum[6 + th][:, :], AF.Abs,
                          bias=r_bdt[:, dc:dc + 1],
                          reads=[b_ps[6 + th], b_bdt], writes=[b_spB[th]])
                for th in range(2):
                    P_act(map_[th], map_[th], AF.Exp, scale=-1.0,
                          reads=[b_spB[th]], writes=[b_spB[th]])
                for th in range(2):
                    P_act(map_[th], map_[th], AF.Ln, bias=1.0,
                          reads=[b_spB[th]], writes=[b_spB[th]])
                for th in range(2):
                    P_tt(slotB[:, dci * 1024 + th * 512:
                               dci * 1024 + (th + 1) * 512],
                         aap[th], map_[th], OP.add,
                         reads=[b_spA[th], b_spB[th]], writes=[b_dlt[dci]],
                         tag="softplus")
            for dci in range(8):
                dc = half * 8 + dci
                P_tt(xib(dc), xib(dc),
                     slotB[:, dci * 1024:(dci + 1) * 1024], OP.mult,
                     reads=[b_xic[dc], b_dlt[dci]], writes=[b_xic[dc]], tag="u")
            if half == 0:
                tap("dbg_delta0", slotB[:, :], list(b_dlt))
                tap("dbg_u", xi_sb[:, :], list(b_xic))

            iters = [(n, dci) for n in range(NST) for dci in range(8)]
            NI = len(iters)

            def ap_ring(it):
                rg = it % 2
                return (rg,
                        xnfm[:, (rg * 3 + 0) * 1024:(rg * 3 + 1) * 1024],
                        xnfm[:, (rg * 3 + 1) * 1024:(rg * 3 + 2) * 1024],
                        xnfm[:, (rg * 3 + 2) * 1024:(rg * 3 + 3) * 1024])

            def em_dA(it):
                n, dci = iters[it]
                dc = half * 8 + dci
                rg, apA, apB, apH = ap_ring(it)
                P_act(apA, slotB[:, dci * 1024:(dci + 1) * 1024], AF.Exp,
                      scale=r_A[:, dc * NST + n: dc * NST + n + 1],
                      reads=[b_dlt[dci], b_A], writes=[b_dA2[rg]], tag="dAexp")

            def em_dB(it):
                n, dci = iters[it]
                dc = half * 8 + dci
                rg, apA, apB, apH = ap_ring(it)
                P_tt(apB, xib(dc), BrepAP(n % 2), OP.mult,
                     reads=[b_xic[dc], b_Brep[n % 2]], writes=[b_dB2[rg]],
                     tag="dB")

            def em_scan(it):
                n, dci = iters[it]
                dc = half * 8 + dci
                rg, apA, apB, apH = ap_ring(it)
                init = 0.0 if tb == 0 else r_carry[:, dc * NST + n:
                                                   dc * NST + n + 1]
                rd = [b_dA2[rg], b_dB2[rg]] + ([] if tb == 0 else [b_carry])
                S.op("DVE", lambda eng, i=init, a=apA, b=apB, hh=apH:
                     eng.tensor_tensor_scan(hh, a, b, i, OP.mult, OP.add),
                     reads=rd, writes=[b_h2[rg]], tag="scan")

            def em_hc(it, acc_now=True):
                n, dci = iters[it]
                dc = half * 8 + dci
                rg, apA, apB, apH = ap_ring(it)
                ya = slotA[:, dc * 1024:(dc + 1) * 1024]
                if n == 0:
                    P_tt(ya, apH, CrepAP(0), OP.mult,
                         reads=[b_h2[rg], b_Crep[0]], writes=[b_slotA])
                else:
                    tr = it % 2
                    apT = xnfm[:, (6 + tr) * 1024:(7 + tr) * 1024]
                    P_tt(apT, apH, CrepAP(n % 2), OP.mult,
                         reads=[b_h2[rg], b_Crep[n % 2]], writes=[b_tmp2[tr]])
                if tb == 0:
                    S.op("DVE", lambda eng, o=r_carry[:, dc * NST + n:
                                                      dc * NST + n + 1],
                         i=apH[:, 1023:1024]: eng.tensor_copy(o, i),
                         reads=[b_h2[rg]], writes=[b_carry], tag="carry")
                if acc_now:
                    em_acc(it)

            def em_acc(it):
                n, dci = iters[it]
                dc = half * 8 + dci
                if n == 0:
                    return
                tr = it % 2
                apT = xnfm[:, (6 + tr) * 1024:(7 + tr) * 1024]
                ya = slotA[:, dc * 1024:(dc + 1) * 1024]
                P_tt(ya, ya, apT, OP.add,
                     reads=[b_slotA, b_tmp2[tr]], writes=[b_slotA],
                     tag="yacc")

            em_bc(0)
            em_rep(0)
            for it in range(NI):
                n, dci = iters[it]
                if dci == 0 and n + 1 < NST:
                    em_bc(n + 1)
                if dci == 2 and n + 1 < NST:
                    em_rep(n + 1)
                em_dA(it)
                if it == 0:
                    em_dB(0)
                if it + 1 < NI:
                    em_dB(it + 1)
                em_scan(it)
                if it > 0:
                    em_hc(it - 1, acc_now=False)
                if it > 1:
                    em_acc(it - 2)
            em_hc(NI - 1, acc_now=False)
            em_acc(NI - 2)
            em_acc(NI - 1)
        tap("dbg_y", slotA[:, :], [b_slotA])
        for dc in range(DCH):
            S.op("POOL", lambda eng,
                 y=slotA[:, dc * 1024:(dc + 1) * 1024],
                 z=sz_sb[:, dc * 1024:(dc + 1) * 1024]:
                 eng.tensor_tensor(y, y, z, OP.mult),
                 reads=[b_slotA, b_szc[dc]], writes=[b_slotA], tag="gate")
        S.alias([b_slotB], b_dlt)
        S.alias([b_x1], b_Brep + b_Crep)
        S.alias([b_xnfm], b_dA2 + b_dB2 + b_h2 + b_tmp2)
        S.alias([b_faH[0], b_faH[1]], b_spA)
        S.alias([b_ob], b_spB)
        def wout_dma(kc2):
            h = kc2 % 4
            P_dma(w_wout[:, h * 1024:(h + 1) * 1024],
                  t_woutbf[(kc2 % DCH) * P:((kc2 % DCH) + 1) * P, :],
                  [b_wwout[h]], reads=[b_wwout[h]])

        def xb_dma(tt):
            P_dma(dA_(0), t_x[sq, t0 + tt * P: t0 + (tt + 1) * P, :],
                  [b_xt[0]])

        for pf in range(3):
            wout_dma(pf)
        for ttg in range(2):
            for kc in range(DCH):
                kc2 = ttg * DCH + kc
                if kc2 + 3 < 2 * DCH:
                    wout_dma(kc2 + 3)
                h = kc2 % 4
                for tti in range(4):
                    tt = ttg * 4 + tti
                    for fh in range(2):
                        pb = psum[tti * 2 + fh]
                        P_mm(pb[:, :],
                             slotA[:, kc * 1024 + tt * P: kc * 1024 + (tt + 1) * P],
                             w_wout[:, h * 1024 + fh * 512:
                                    h * 1024 + (fh + 1) * 512],
                             kc == 0, kc == DCH - 1,
                             reads=[b_slotA, b_wwout[h]],
                             writes=[b_ps[tti * 2 + fh]])
            xb_dma(ttg * 4)
            for tti in range(4):
                tt = ttg * 4 + tti
                P_ts(dA_(2), dA_(0), mu1a[:, tt:tt + 1], r1a[:, tt:tt + 1],
                     OP.subtract, OP.mult,
                     reads=[b_xt[0], b_mu1a, b_r1a], writes=[b_ob])
                for fh in range(2):
                    pb = psum[tti * 2 + fh]
                    P_tt(slotB[:, tt * 1024 + fh * 512:
                               tt * 1024 + (fh + 1) * 512],
                         dA_(2)[:, fh * 512:(fh + 1) * 512], pb[:, :], OP.add,
                         reads=[b_ob, b_ps[tti * 2 + fh]], writes=[b_slotB],
                         tag="t2")
                t2ap = slotB[:, tt * 1024:(tt + 1) * 1024]
                ln_stats(t2ap, [b_slotB], tt, None, None, rrs[:, tt:tt + 1],
                         b_rrs[tt])
                mu_ap, mu_buf = mean_of(tt)
                P_ts(dA_(2), t2ap, mu_ap, rrs[:, tt:tt + 1], OP.subtract,
                     OP.mult, reads=[b_slotB, mu_buf, b_rrs[tt]],
                     writes=[b_ob])
                P_tt(slotC[:, tt * 1024:(tt + 1) * 1024], dA_(2), dA_(0),
                     OP.add, reads=[b_ob, b_xt[0]], writes=[b_x1], tag="x1")
                if tti + 1 < 4:
                    xb_dma(ttg * 4 + tti + 1)
        tap("dbg_x1", slotC[:, :], [b_x1])
        for tt in range(NT):
            x1ap = slotC[:, tt * 1024:(tt + 1) * 1024]
            ln_stats(x1ap, [b_x1], tt, None, None, rrs[:, tt:tt + 1],
                     b_rrs[tt])
            r16 = stt[:, tt * 16 + 14: tt * 16 + 15]
            S.op("ACT", lambda eng, o=r16, i=rrs[:, tt:tt + 1]: eng.mul(
                o, i, 16.0), reads=[b_rrs[tt]], writes=[b_stat[tt]])
            mu_ap, mu_buf = mean_of(tt)
            P_ts(dA_(1), x1ap, mu_ap, r16, OP.subtract,
                 OP.mult, reads=[b_x1, mu_buf, b_stat[tt]],
                 writes=[b_faH[0], b_faH[1]])
            P_ts(dA_(1), dA_(1), RC, RC, OP.add, OP.subtract,
                 reads=[b_faH[0], b_faH[1]], writes=[b_faH[0], b_faH[1]])
            P_ts(dA_(1), dA_(1), -128.0, 127.0, OP.max, OP.min,
                 reads=[b_faH[0], b_faH[1]], writes=[b_faH[0], b_faH[1]])
            for g in range(2):
                pb = psum[g]
                for j in range(4):
                    dc = g * 4 + j
                    P_tp(pb[:, j * P:(j + 1) * P], dA_(1)[:, dc * P:(dc + 1) * P],
                         reads=[b_faH[0], b_faH[1]], writes=[b_ps[g]])
                for j in range(4):
                    dc = g * 4 + j
                    P_act(xnfm[:, dc * 1024 + tt * P: dc * 1024 + (tt + 1) * P],
                          pb[:, j * P:(j + 1) * P], AF.Identity,
                          reads=[b_ps[g]], writes=[b_xnfm])

        def w1_dma(hc2):
            h = hc2 % 4
            P_dma(w_w1[:, h * 1024:(h + 1) * 1024]
                  .rearrange("p (kc f) -> p kc f", f=P),
                  t_w1q[hc2 % HC].rearrange("(kc p) f -> p kc f", p=P),
                  [b_ww1[h]], reads=[b_ww1[h]])

        def w2_dma(kc2):
            h = kc2 % 4
            P_dma(w_w2[:, h * 1024:(h + 1) * 1024],
                  t_w2q[(kc2 % HC) * P:((kc2 % HC) + 1) * P, :],
                  [b_ww2[h]], reads=[b_ww2[h]])

        for pf in range(3):
            w1_dma(pf)
        for th in range(2):
            for hc in range(HC):
                hc2 = th * HC + hc
                if hc2 + 3 < 2 * HC:
                    w1_dma(hc2 + 3)
                h = hc2 % 4
                q = hc % 2
                pb = psum[2 + q]
                for kc in range(KCH):
                    P_mm(pb[:, :], w_w1[:, h * 1024 + kc * P:
                                        h * 1024 + (kc + 1) * P],
                         xnfm[:, kc * 1024 + th * 512: kc * 1024 + (th + 1) * 512],
                         kc == 0, kc == KCH - 1,
                         reads=[b_ww1[h], b_xnfm], writes=[b_ps[2 + q]])
                fap = dA_(1)[:, q * 512:(q + 1) * 512]
                P_act(fap, pb[:, :], AF.Identity,
                      scale=1.0 / 64.0, bias=r_b1q4[:, hc:hc + 1],
                      reads=[b_ps[2 + q], b_b1q4], writes=[b_faH[q]])
                P_ts(fap, fap, RC, RC, OP.add, OP.subtract,
                     reads=[b_faH[q]], writes=[b_faH[q]])
                P_ts(slotA[:, hc * 512:(hc + 1) * 512], fap,
                     0.0, 127.0, OP.max, OP.min, reads=[b_faH[q]],
                     writes=[b_slotA])
            for pf in range(3):
                w2_dma(th * HC + pf)
            for kc in range(HC):
                kc2 = th * HC + kc
                if kc2 + 3 < 2 * HC:
                    w2_dma(kc2 + 3)
                h = kc2 % 4
                for tti in range(4):
                    for fh in range(2):
                        pb = psum[tti * 2 + fh]
                        P_mm(pb[:, :],
                             slotA[:, kc * 512 + tti * P: kc * 512 + (tti + 1) * P],
                             w_w2[:, h * 1024 + fh * 512:
                                  h * 1024 + (fh + 1) * 512],
                             kc == 0, kc == HC - 1,
                             reads=[b_slotA, b_ww2[h]],
                             writes=[b_ps[tti * 2 + fh]])
            for tti in range(4):
                tt = th * 4 + tti
                for fh in range(2):
                    pb = psum[tti * 2 + fh]
                    S.op("DVE", lambda eng, o=dA_(1)[:, fh * 512:(fh + 1) * 512],
                         p=pb, f=fh: eng.scalar_tensor_tensor(
                             out=o, in0=p[:, :], scalar=1.0 / 1024.0,
                             in1=r_b2qrep[:, f * 512:(f + 1) * 512],
                             op0=OP.mult, op1=OP.add),
                         reads=[b_ps[tti * 2 + fh], b_b2qrep],
                         writes=[b_faH[fh]], tag="ffn2evac")
                ln_stats(dA_(1), [b_faH[0], b_faH[1]], tt, None, None,
                         rrs[:, tt:tt + 1], b_rrs[tt])
                mu_ap, mu_buf = mean_of(tt)
                P_ts(dA_(2), dA_(1), mu_ap, rrs[:, tt:tt + 1], OP.subtract,
                     OP.mult, reads=[b_faH[0], b_faH[1], mu_buf, b_rrs[tt]],
                     writes=[b_ob])
                P_tt(dA_(2), dA_(2), slotC[:, tt * 1024:(tt + 1) * 1024],
                     OP.add, reads=[b_ob, b_x1], writes=[b_ob], tag="out")
                S.dma("SP", lambda eng, a=dA_(2),
                      o=t_out[sq, t0 + tt * P: t0 + (tt + 1) * P, :]:
                      eng.dma_start(out=o, in_=a),
                      reads=[b_ob], writes=[], tag="outdma")
        S.alias([b_xn1], [b_faH[0], b_faH[1]])
        S.alias([b_xn2f], [b_ob])

    S.barrier()
    S.op("SP", lambda eng: eng.nop(), tag="final")
    S.emit(block)
    ex.close()
    return nc, "out"


def kernel(**inputs):
    shards = _stage(inputs)
    nc, out_name = build_program()
    res = run_bass_kernel_spmd(nc, shards, core_ids=list(range(8)))
    out = np.empty((16, L, D), np.float32)
    for c, r in enumerate(res.results):
        out[2 * c:2 * c + 2] = r[out_name]
    return out
